# version 08
# baseline (speedup 1.0000x reference)
import numpy as np
import ml_dtypes
import concourse.bass as bass
import concourse.mybir as mybir
from concourse.bass_utils import run_bass_kernel_spmd

F32 = mybir.dt.float32
BF16 = mybir.dt.bfloat16
I8 = mybir.dt.int8
AF = mybir.ActivationFunctionType
ALU = mybir.AluOpType

D = 2048
NDC = D // 128
T = 512
NCORE = 8
NSEQ = 2
EPS = 1e-5
RING = 3
SLOT_F = 4096

bfdt = ml_dtypes.bfloat16

_CONST_COLS = {}
_c = 0
for _n, _w in [("n1w", 16), ("n1b", 16), ("n2w", 16), ("n2b", 16),
               ("td", 16), ("tf", 16), ("tmr", 16), ("tmk", 16),
               ("tmv", 16), ("cmm", 16), ("sdown", 16),
               ("sgate", 64), ("sup", 64)]:
    _CONST_COLS[_n] = (_c, _w)
    _c += _w
NCONST = _c


def _vec128(v):
    v = np.asarray(v).reshape(-1)
    return np.ascontiguousarray(v.reshape(-1, 128).T)


def _block_w(w, wm):
    K, M = w.shape
    nk, nm = K // 128, M // wm
    return np.ascontiguousarray(
        w.reshape(nk, 128, nm, wm).transpose(2, 1, 0, 3).reshape(nm, 128, nk * wm))


class _Op:
    __slots__ = ("eng", "emit", "waits", "incs", "nodrain")

    def __init__(self, eng, emit, waits, incs, nodrain=False):
        self.eng, self.emit, self.waits, self.incs = eng, emit, waits, incs
        self.nodrain = nodrain


class Builder:
    def __init__(self, nchunk=4, nseq=NSEQ, debug=False):
        self.nchunk = nchunk
        self.nseq = nseq
        self.debug = debug
        nc = bass.Bass("TRN2", target_bir_lowering=False)
        self.nc = nc
        self.ops = []
        self.cnt = {}
        self.sems = {}
        semnames = ["qx", "qc", "qo", "qs", "tp", "tv", "ta", "tg"] + [
            f"qw{i}" for i in range(RING)]
        for s in semnames:
            self.sems[s] = nc.alloc_semaphore(s)
            self.cnt[s] = 0

        dt = nc.dram_tensor
        self.x5 = dt("x5", [nseq, nchunk, 128, NDC * T], F32, kind="ExternalInput").ap()
        self.out5 = dt("out5", [nseq, nchunk, 128, NDC * T], F32,
                       kind="ExternalOutput").ap()
        self.wdram = {}
        for name, shape, dtype in [
            ("wr", [8, 128, 4096], F32), ("wk", [8, 128, 4096], F32),
            ("wv", [8, 128, 4096], F32), ("wo", [8, 128, 4096], F32),
            ("wcr", [8, 128, 4096], F32), ("wck", [24, 128, 4096], F32),
            ("wcv_lo", [16, 128, 3072], F32), ("wcv_hi", [16, 128, 3072], F32),
            ("wg", [32, 128, 4096], I8), ("wu", [32, 128, 4096], I8),
            ("wd_lo", [16, 128, 4096], I8), ("wd_hi", [16, 128, 4096], I8),
        ]:
            self.wdram[name] = dt(name, shape, dtype, kind="ExternalInput").ap()
        self.scratch = {}
        for name in ("wr", "wk", "wv", "wo", "wcr", "wck",
                     "wcv_lo", "wcv_hi"):
            shape = list(self.wdram[name].shape)
            self.scratch[name] = dt("sc_" + name, shape, BF16,
                                    kind="Internal").ap()
        self.constsd = dt("consts", [128, NCONST], F32, kind="ExternalInput").ap()
        self.onesbd = dt("onesb", [128, 1], BF16, kind="ExternalInput").ap()
        self.onesrd = dt("onesr", [1, 128], F32, kind="ExternalInput").ap()

        al = nc.alloc_sbuf_tensor
        self.ring = [al(f"ring{i}", [128, SLOT_F], BF16).ap() for i in range(RING)]
        self.xh = al("xh", [128, 8192], BF16).ap()
        self.xn = al("xn", [128, 8192], BF16).ap()
        self.fuse = al("fuse", [128, 16384], BF16).ap()
        self.region = al("region", [128, 24576], BF16).ap()
        self.wkvmix = al("wkvmix", [128, 8192], BF16).ap()
        self.acc = al("acc", [128, 8192], F32).ap()
        self.consts = al("consts_t", [128, NCONST], F32).ap()
        self.onesb = al("onesb_t", [128, 1], BF16).ap()
        self.onesr = al("onesr_t", [1, 128], F32).ap()
        self.ed = al("ed_t", [128, 16], F32).ap()
        self.wdec = al("wdec_t", [128, 16], F32).ap()
        self.efa = al("efa_t", [128, 16], F32).ap()
        self.carry = [al(f"carry{s}", [128, 16], F32).ap() for s in range(nseq)]
        self.h1halo = [al(f"h1halo{s}", [128, 16], BF16).ap() for s in range(nseq)]
        self.h2halo = [al(f"h2halo{s}", [128, 16], BF16).ap() for s in range(nseq)]
        self.rB = al("rB", [128, T], F32).ap()
        self.mrB = al("mrB", [128, T], F32).ap()
        self.vecs = al("vecs", [1, T], F32).ap()
        self.rsig = [al(f"rsig{i}", [128, T], F32).ap() for i in range(2)]
        self.kt = [al(f"kt{i}", [128, T], F32).ap() for i in range(2)]
        self.kvt = al("kvt", [128, T], F32).ap()
        self.gatet = [al(f"gatet{i}", [128, T], BF16).ap() for i in range(2)]
        self.ut = al("ut_t", [128, 16], F32).ap()
        self.d0t = al("d0t", [128, 16], BF16).ap()
        self.t3 = al("t3", [128, T], F32).ap()
        self.epsc = al("epsc", [1, 1], F32).ap()

        ps = nc.alloc_psum_tensor
        self.pA = [ps(f"pA{i}", [128, T], F32).ap() for i in range(2)]
        self.pB = [ps(f"pB{i}", [128, T], F32).ap() for i in range(2)]
        self.pD = [ps(f"pD{i}", [128, T], F32).ap() for i in range(2)]
        self.pS0 = ps("pS0", [1, T], F32).ap()
        self.pS1 = ps("pS1", [1, T], F32).ap()

        self.bank_free = {}
        self.slab_idx = 0
        self.slab_consumed = {}
        self.slab_store = {}
        self.cur_ci = 0
        self.scratch_wait_done = False
        self.nx_dma = 0
        self.nout = 0
        self.dbg_out = {}
        self.ta_setup = 0
        self.tv_xn_pending = 0
        self.tg_q_done = 0

    def op(self, eng, emit, waits=(), inc=None, inc16=None, nodrain=False):
        wl = []
        for (s, v) in waits:
            assert v <= self.cnt[s], f"wait on future {s}: {v} > {self.cnt[s]}"
            if v > 0:
                wl.append((s, v))
        incs = []
        if inc:
            self.cnt[inc] += 1
            incs.append((inc, 1))
        if inc16:
            self.cnt[inc16] += 16
            incs.append((inc16, 16))
        self.ops.append(_Op(eng, emit, wl, incs, nodrain=nodrain))
        return self.cnt[incs[0][0]] if incs else None

    def gop(self, emit, waits=()):
        return self.op("gpsimd", emit, waits=waits, inc="tg", nodrain=True)

    def c(self, s):
        return self.cnt[s]

    def cv(self, name, i=None, n=1):
        c0, w = _CONST_COLS[name]
        if i is None:
            return self.consts[:, c0:c0 + w]
        return self.consts[:, c0 + i:c0 + i + n]

    def finalize(self):
        nc = self.nc
        eng_ops = {"sync": [], "scalar": [], "vector": [], "tensor": [], "gpsimd": []}
        for o in self.ops:
            eng_ops[o.eng].append(o)

        def body(eng, name):
            do_drain = name in ("vector", "scalar")
            guaranteed = {}
            for o in eng_ops[name]:
                for (s, v) in o.waits:
                    if guaranteed.get(s, 0) >= v:
                        continue
                    eng.wait_ge(self.sems[s], v)
                    guaranteed[s] = v
                inst = o.emit(eng)
                for (s, v) in o.incs:
                    inst.then_inc(self.sems[s], v)
                if do_drain and not o.nodrain:
                    eng.drain()

        with nc.Block() as blk:
            blk.sync(lambda e: body(e, "sync"))
            blk.scalar(lambda e: body(e, "scalar"))
            blk.vector(lambda e: body(e, "vector"))
            blk.tensor(lambda e: body(e, "tensor"))
            blk.gpsimd(lambda e: body(e, "gpsimd"))
        return nc

    def setup(self):
        s = self
        s.op("sync", lambda e: e.dma_start(out=s.consts, in_=s.constsd), inc16="qc")
        s.op("sync", lambda e: e.dma_start(out=s.onesb, in_=s.onesbd), inc16="qc")
        s.op("sync", lambda e: e.dma_start(out=s.onesr, in_=s.onesrd), inc16="qc")
        qc = s.c("qc")
        s.x_load(0)
        s.op("tensor", lambda e: e.nop(), waits=[("qc", qc)])
        s.op("scalar", lambda e: e.activation(s.ed, s.cv("td"), AF.Exp),
             waits=[("qc", qc)], inc="ta")
        s.op("scalar", lambda e: e.activation(s.wdec, s.ed, AF.Exp, scale=-1.0),
             inc="ta")
        s.op("scalar", lambda e: e.activation(s.efa, s.cv("tf"), AF.Exp), inc="ta")
        s.ta_setup = s.c("ta")
        for t in s.carry + s.h1halo + s.h2halo:
            s.op("vector", (lambda t_: lambda e: e.memset(t_, 0.0))(t))
        s.op("vector", lambda e: e.memset(s.epsc, EPS))
        s.op("vector", lambda e: e.engine_nop(), inc="tv")

    def x_load(self, ci, waits=()):
        s = self
        c, sq = divmod(ci, s.nseq)
        s.op("gpsimd",
             (lambda sr: lambda e: e.dma_start(out=s.xh, in_=sr))(s.x5[sq, c]),
             waits=list(waits), inc16="qx", nodrain=True)
        s.nx_dma += 1

    def slab_dma(self, wname, m, used_elems=SLOT_F):
        s = self
        j = s.slab_idx
        s.slab_idx += 1
        slot = j % RING
        waits = []
        if j - RING in s.slab_consumed:
            waits.append(("tp", s.slab_consumed[j - RING]))
        if j - RING in s.slab_store:
            waits.append(("qs", s.slab_store[j - RING]))
        dst = s.ring[slot][:, 0:used_elems]
        sem = f"qw{slot}"
        fam32 = wname in s.scratch
        if fam32 and s.cur_ci > 0:
            if not s.scratch_wait_done:
                waits.append(("qs", s.c("qs")))
                s.scratch_wait_done = True
            src = s.scratch[wname][m]
            tgt = s.op("sync",
                       (lambda d2, so: lambda e: e.dma_start(out=d2, in_=so))(
                           dst, src),
                       waits=waits, inc16=sem)
        else:
            src = s.wdram[wname][m]
            tgt = s.op("gpsimd",
                       (lambda d2, so: lambda e: e.dma_start(out=d2, in_=so))(
                           dst, src),
                       waits=waits, inc16=sem, nodrain=True)
            if fam32:
                stgt = s.op("sync",
                            (lambda d2, so: lambda e: e.dma_start(
                                out=so, in_=d2))(dst, s.scratch[wname][m]),
                            waits=[(sem, tgt)], inc16="qs")
                s.slab_store[j] = stgt
        return {"j": j, "slot": slot, "wait": (sem, tgt)}

    def mm_group(self, bank_key, bank_ap, pairs, slabs, extra_waits=()):
        s = self
        waits = list(extra_waits)
        for tok in slabs:
            waits.append(tok["wait"])
        bf = s.bank_free.get(bank_key)
        if bf:
            waits.append(bf)
        n = len(pairs)
        gid = None
        for i, (lh, rh) in enumerate(pairs):
            st, sp = (i == 0), (i == n - 1)
            gid = s.op("tensor",
                       (lambda l, r, st_, sp_, b: lambda e: e.matmul(
                           b, l, r, start=st_, stop=sp_))(lh, rh, st, sp, bank_ap),
                       waits=waits if i == 0 else (),
                       inc="tp" if sp else None)
        for tok in slabs:
            s.slab_consumed[tok["j"]] = s.c("tp")
        return gid

    def evac(self, eng_sem, bank_key, gid, emit, extra_waits=()):
        s = self
        eng = {"tv": "vector", "ta": "scalar"}[eng_sem]
        waits = [("tp", gid)] + list(extra_waits)
        cnt = s.op(eng, emit, waits=waits, inc=eng_sem)
        s.bank_free[bank_key] = (eng_sem, cnt)
        return cnt

    def ln_stats(self, src_bf, sq_dst, ready, extra_stat=()):
        s = self
        x3 = src_bf.rearrange("p (a b) -> p a b", b=T)
        sq3 = sq_dst.rearrange("p (a b) -> p a b", b=T)
        tvsq = s.op("vector", lambda e: e.tensor_mul(sq3, x3, x3),
                    waits=list(ready), inc="tv")
        w0 = list(ready)
        if "S0" in s.bank_free:
            w0.append(s.bank_free["S0"])
        for dc in range(NDC):
            s.op("tensor",
                 (lambda d: lambda e: e.matmul(
                     s.pS0, s.onesb, src_bf[:, d * T:(d + 1) * T],
                     start=(d == 0), stop=(d == NDC - 1)))(dc),
                 waits=w0 if dc == 0 else ())
        w1 = [("tv", tvsq)]
        if "S1" in s.bank_free:
            w1.append(s.bank_free["S1"])
        gid = None
        for dc in range(NDC):
            gid = s.op("tensor",
                       (lambda d: lambda e: e.matmul(
                           s.pS1, s.onesb, sq_dst[:, d * T:(d + 1) * T],
                           start=(d == 0), stop=(d == NDC - 1)))(dc),
                       waits=w1 if dc == 0 else (),
                       inc="tp" if dc == NDC - 1 else None)
        vm = s.rB[0:1, :]
        vt = s.mrB[0:1, :]
        vmr = s.vecs
        s.op("vector", lambda e: e.tensor_scalar_mul(vm, s.pS0, 1.0 / D),
             waits=[("tp", gid)] + list(extra_stat))
        s.op("vector", lambda e: e.tensor_mul(vmr, vm, vm))
        tvv = s.op("vector",
                   lambda e: e.scalar_tensor_tensor(
                       vt, s.pS1, 1.0 / D, vmr, ALU.mult, ALU.subtract),
                   inc="tv")
        s.bank_free["S0"] = ("tv", tvv)
        s.bank_free["S1"] = ("tv", tvv)
        tav = s.op("scalar", lambda e: e.activation(vt, vt, AF.Sqrt, bias=s.epsc),
                   waits=[("tv", tvv)], inc="ta")
        s.op("vector", lambda e: e.reciprocal(vt, vt), waits=[("ta", tav)])
        tvr = s.op("vector", lambda e: e.tensor_mul(vmr, vm, vt), inc="tv")

        def finish(extra_copy=()):
            bc_waits = [("tv", tvr)]
            if ("D", 0) in s.bank_free:
                bc_waits.append(s.bank_free[("D", 0)])
            gid0 = s.op("tensor",
                        lambda e: e.matmul(s.pD[0], s.onesr, vt, start=True,
                                           stop=True),
                        waits=bc_waits, inc="tp")
            bc1_waits = []
            if ("D", 1) in s.bank_free:
                bc1_waits.append(s.bank_free[("D", 1)])
            gid1 = s.op("tensor",
                        lambda e: e.matmul(s.pD[1], s.onesr, vmr, start=True,
                                           stop=True),
                        waits=bc1_waits, inc="tp")
            tv1 = s.op("vector", lambda e: e.tensor_copy(s.rB, s.pD[0]),
                       waits=[("tp", gid0)], inc="tv")
            ta1 = s.op("scalar", lambda e: e.activation(s.mrB, s.pD[1], AF.Copy),
                       waits=[("tp", gid1)] + list(extra_copy), inc="ta")
            s.bank_free[("D", 0)] = ("tv", tv1)
            s.bank_free[("D", 1)] = ("ta", ta1)
            return tv1, ta1
        return finish

    def emit_preln(self, ci, ready_tp=None, extra_stat=(), apply_wait_tg=0):
        s = self
        ready = [("qx", 16 * (ci + 1))]
        if ci == 0:
            ready.append(("qc", s.c("qc")))
        if ready_tp is not None:
            ready.append(ready_tp)
        x_bf = s.xh[:, 0:8192]
        fin = s.ln_stats(x_bf, s.wkvmix[:, 0:8192], ready, extra_stat=extra_stat)
        tv1, ta1 = fin(extra_copy=list(extra_stat))
        x3 = x_bf.rearrange("p (a b) -> p a b", b=T)
        xn3 = s.xn.rearrange("p (a b) -> p a b", b=T)

        def bc3(t):
            return t.unsqueeze(1).broadcast_to((128, NDC, T))
        aw = []
        if apply_wait_tg:
            aw.append(("tg", apply_wait_tg))
        s.op("vector", lambda e: e.tensor_mul(xn3, x3, bc3(s.rB)), waits=aw)
        s.tv_xn_pending = s.op("vector",
                               lambda e: e.tensor_sub(xn3, xn3, bc3(s.mrB)),
                               waits=[("ta", ta1)], inc="tv")

    def dve_ln_mix(self, tm_items, nw, nb, halo, tv1, ta1, tp_guard):
        s = self
        q3 = s.region[:, 0:8192].rearrange("p (a b) -> p a b", b=T)
        xn3 = s.xn.rearrange("p (a b) -> p a b", b=T)
        xh3 = s.xh.rearrange("p (a b) -> p a b", b=T)

        def bc3(t):
            return t.unsqueeze(1).broadcast_to((128, NDC, T))

        def bci(t16, n=T):
            return t16.unsqueeze(2).broadcast_to((128, NDC, n))
        w0 = [("ta", ta1)]
        if tp_guard:
            w0.append(("tp", tp_guard))
        s.op("vector", lambda e: e.tensor_mul(q3, xn3, bc3(s.rB)), waits=w0)
        yield
        s.tv_q_done = s.op("vector",
                           lambda e: e.tensor_sub(q3, q3, bc3(s.mrB)),
                           inc="tv")
        yield
        s.op("vector", lambda e: e.tensor_mul(xh3, q3, bci(s.cv(nw))))
        yield
        s.tv_h_done = s.op("vector",
                           lambda e: e.tensor_add(xh3, xh3, bci(s.cv(nb))),
                           inc="tv")
        yield
        s.op("vector", lambda e: e.tensor_sub(s.d0t, xh3[:, :, 0], halo))
        yield
        first3 = tm_items[0][0]
        s.op("vector", lambda e: e.tensor_sub(
            first3[:, :, 1:512], xh3[:, :, 1:512], xh3[:, :, 0:511]))
        yield
        for (dst3_, tmkey) in tm_items:
            c0 = dst3_[:, :, 0]
            s.op("vector", (lambda c0_, tk: lambda e: e.tensor_mul(
                c0_, s.d0t, s.cv(tk)))(c0, tmkey))
            yield
            s.op("vector", (lambda c0_: lambda e: e.tensor_add(
                c0_, c0_, halo))(c0))
            yield
        s.op("vector", lambda e: e.tensor_copy(halo, xh3[:, :, 511]))
        yield
        for (dst3_, tmkey) in tm_items[1:]:
            s.op("vector", (lambda ds, tk: lambda e: e.tensor_mul(
                ds[:, :, 1:512], first3[:, :, 1:512],
                bci(s.cv(tk), 511)))(dst3_, tmkey))
            yield
        tk0 = tm_items[0][1]
        s.op("vector", (lambda tk: lambda e: e.tensor_mul(
            first3[:, :, 1:512], first3[:, :, 1:512],
            bci(s.cv(tk), 511)))(tk0))
        yield
        for (dst3_, tmkey) in tm_items:
            s.op("vector", (lambda ds: lambda e: e.tensor_add(
                ds[:, :, 1:512], ds[:, :, 1:512], xh3[:, :, 0:511]))(dst3_),
                inc="tv")
            yield

    @staticmethod
    def drip(gen, n):
        for _ in range(n):
            try:
                next(gen)
            except StopIteration:
                return False
        return True

    def ffn_gu(self, f_range, tv_xn, state):
        s = self
        for f in f_range:
            par = f % 2
            if f % 2 == 0:
                state["jg"] = s.slab_dma("wg", f // 2)
                state["ju"] = s.slab_dma("wu", f // 2)
            jg, ju = state["jg"], state["ju"]
            ringg, ringu = s.ring[jg["slot"]], s.ring[ju["slot"]]
            fd = f % 2
            pairsg = [(ringg[:, kc * 256 + fd * 128: kc * 256 + fd * 128 + 128],
                       s.xn[:, kc * T:(kc + 1) * T]) for kc in range(16)]
            gidg = s.mm_group(("A", par), s.pA[par], pairsg, [jg],
                              extra_waits=[("tv", tv_xn)])
            ta_g = s.evac("ta", ("A", par), gidg,
                          (lambda par_, f_: lambda e: e.activation(
                              s.gatet[par_], s.pA[par_], AF.Silu,
                              scale=s.cv("sgate", f_)))(par, f),
                          extra_waits=([("tv", state["gate_free"][par])]
                                       if par in state["gate_free"] else ()))
            pairsu = [(ringu[:, kc * 256 + fd * 128: kc * 256 + fd * 128 + 128],
                       s.xn[:, kc * T:(kc + 1) * T]) for kc in range(16)]
            gidu = s.mm_group(("B", par), s.pB[par], pairsu, [ju])
            fs = s.fuse[:, (f % 32) * T:((f % 32) + 1) * T]
            tvf = s.evac("tv", ("B", par), gidu,
                         (lambda par_, f_, fs_: lambda e: e.scalar_tensor_tensor(
                             fs_, s.pB[par_], s.cv("sup", f_),
                             s.gatet[par_], ALU.mult, ALU.mult))(par, f, fs),
                         extra_waits=[("ta", ta_g)])
            state["gate_free"][par] = tvf

    def ffn_down(self, half, tv_fuse, first_extra=()):
        s = self
        wname = "wd_lo" if half == 0 else "wd_hi"
        for dout in range(16):
            par = dout % 2
            jl = s.slab_dma(wname, dout)
            rl = s.ring[jl["slot"]]
            pairs = [(rl[:, kc * 128:(kc + 1) * 128],
                      s.fuse[:, kc * T:(kc + 1) * T]) for kc in range(32)]
            gid = s.mm_group(("A", par), s.pA[par], pairs, [jl],
                             extra_waits=[("tv", tv_fuse)])
            ew = list(first_extra) if dout == 0 else []
            if half == 0:
                x_bf = s.xh[:, 0:8192]
                emit = (lambda par_, d_: lambda e: e.scalar_tensor_tensor(
                    s.acc[:, d_ * T:(d_ + 1) * T], s.pA[par_],
                    s.cv("sdown", d_), x_bf[:, d_ * T:(d_ + 1) * T],
                    ALU.mult, ALU.add))(par, dout)
            else:
                emit = (lambda par_, d_: lambda e: e.scalar_tensor_tensor(
                    s.acc[:, d_ * T:(d_ + 1) * T], s.pA[par_],
                    s.cv("sdown", d_), s.acc[:, d_ * T:(d_ + 1) * T],
                    ALU.mult, ALU.add))(par, dout)
            s.evac("tv", ("A", par), gid, emit, extra_waits=ew)
            yield

    def chunk(self, ci):
        s = self
        s.cur_ci = ci
        c, sq = divmod(ci, s.nseq)
        xn3 = s.xn.rearrange("p (a b) -> p a b", b=T)
        acc3 = s.acc.rearrange("p (a b) -> p a b", b=T)
        xh3 = s.xh.rearrange("p (a b) -> p a b", b=T)
        reg = s.region
        last = s.nchunk * s.nseq - 1

        if ci == 0:
            s.emit_preln(0)
        tv_xn = s.tv_xn_pending
        tp_chunk_start = s.c("tp")

        gu_state = {"gate_free": {}, "jg": None, "ju": None}
        s.ffn_gu(range(0, 32), tv_xn, gu_state)
        tv_fuseA = s.c("tv")

        fin1 = s.ln_stats(s.xn, s.wkvmix[:, 0:8192],
                          [("tv", tv_xn), ("tp", tp_chunk_start)])

        ew0 = [("qo", 16 * s.nout)] if s.nout > 0 else []
        downA = s.ffn_down(0, tv_fuseA, first_extra=ew0)
        for _ in range(16):
            next(downA)
        tp_downA = s.c("tp")
        tv_downA = s.c("tv")

        mixr3 = reg[:, 0:8192].rearrange("p (a b) -> p a b", b=T)
        mixk3 = reg[:, 8192:16384].rearrange("p (a b) -> p a b", b=T)
        mixv3 = reg[:, 16384:24576].rearrange("p (a b) -> p a b", b=T)
        g1 = None
        tv1 = ta1 = None
        for i, f in enumerate(range(32, 64)):
            s.ffn_gu([f], tv_xn, gu_state)
            if i == 1:
                tv1, ta1 = fin1()
                g1 = s.dve_ln_mix(
                    [(mixr3, "tmr"), (mixk3, "tmk"), (mixv3, "tmv")],
                    "n1w", "n1b", s.h1halo[sq], tv1, ta1, tp_chunk_start)
            if g1 is not None:
                s.drip(g1, 1)
        while s.drip(g1, 64):
            pass
        tv_mix = s.c("tv")
        tv_fuseB = tv_mix

        mixdst = {"r": reg[:, 0:8192], "k": reg[:, 8192:16384],
                  "v": reg[:, 16384:24576]}
        ring_rkv = {}
        rsig_free = {}
        kt_free = {}
        for dc in range(NDC):
            par = dc % 2
            if dc % 2 == 0:
                q = dc // 2
                for m in ("r", "k", "v"):
                    tok = s.slab_dma("w" + m, q)
                    ring_rkv[m] = (tok, s.ring[tok["slot"]])
            fd = dc % 2

            def lh(rt, kc):
                return rt[:, kc * 256 + fd * 128: kc * 256 + fd * 128 + 128]

            jr, rr = ring_rkv["r"]
            gidr = s.mm_group(("A", par), s.pA[par],
                              [(lh(rr, kc), mixdst["r"][:, kc * T:(kc + 1) * T])
                               for kc in range(16)], [jr],
                              extra_waits=[("tv", tv_mix)])
            s.evac("ta", ("A", par), gidr,
                   (lambda par_: lambda e: e.activation(
                       s.rsig[par_], s.pA[par_], AF.Sigmoid))(par),
                   extra_waits=([("tv", rsig_free[par])]
                                if par in rsig_free else ()))
            jk, rk = ring_rkv["k"]
            gidk = s.mm_group(("B", par), s.pB[par],
                              [(lh(rk, kc), mixdst["k"][:, kc * T:(kc + 1) * T])
                               for kc in range(16)], [jk])
            s.evac("ta", ("B", par), gidk,
                   (lambda par_: lambda e: e.copy(s.kt[par_], s.pB[par_]))(par),
                   extra_waits=([("tv", kt_free[par])]
                                if par in kt_free else ()))
            jv, rv = ring_rkv["v"]
            gidv = s.mm_group(("D", par), s.pD[par],
                              [(lh(rv, kc), mixdst["v"][:, kc * T:(kc + 1) * T])
                               for kc in range(16)], [jv])
            kt_free[par] = s.evac(
                "tv", ("D", par), gidv,
                (lambda par_: lambda e: e.tensor_mul(
                    s.kvt, s.pD[par_], s.kt[par_]))(par),
                extra_waits=[("ta", s.c("ta"))])
            scan_waits = [("ta", s.ta_setup)] if ci == 0 and dc == 0 else ()
            s.op("vector",
                 (lambda d: lambda e: e.tensor_tensor_scan(
                     s.kvt, s.wdec[:, d:d + 1].broadcast_to((128, T)), s.kvt,
                     s.carry[sq][:, d:d + 1], ALU.mult, ALU.add))(dc),
                 waits=scan_waits)
            s.op("vector",
                 (lambda d: lambda e: e.tensor_copy(
                     s.carry[sq][:, d:d + 1], s.kvt[:, T - 1:T]))(dc))
            wkv_dst = s.wkvmix[:, dc * T:(dc + 1) * T]
            rsig_free[par] = s.op(
                "vector",
                (lambda par_, ds: lambda e: e.tensor_mul(
                    ds, s.rsig[par_], s.kvt))(par, wkv_dst), inc="tv")
            if c == 0:
                s.op("vector",
                     (lambda d: lambda e: e.tensor_scalar_mul(
                         s.ut[:, 0:1], s.kvt[:, 0:1], s.efa[:, d:d + 1]))(dc))
                s.op("vector",
                     (lambda par_: lambda e: e.tensor_mul(
                         s.ut[:, 0:1], s.ut[:, 0:1],
                         s.rsig[par_][:, 0:1]))(par))
                rsig_free[par] = s.op(
                    "vector",
                    (lambda ds: lambda e: e.tensor_add(
                        ds[:, 0:1], ds[:, 0:1], s.ut[:, 0:1]))(wkv_dst),
                    inc="tv")
        tv_wkv = s.op("vector", lambda e: e.engine_nop(), inc="tv")

        jo = None
        ringo = None
        for dc in range(NDC):
            par = dc % 2
            if dc % 2 == 0:
                jo = s.slab_dma("wo", dc // 2)
                ringo = s.ring[jo["slot"]]
            fd = dc % 2
            pairs = [(ringo[:, kc * 256 + fd * 128: kc * 256 + fd * 128 + 128],
                      s.wkvmix[:, kc * T:(kc + 1) * T]) for kc in range(16)]
            gid = s.mm_group(("A", par), s.pA[par], pairs, [jo],
                             extra_waits=[("tv", tv_wkv)])
            s.evac("tv", ("A", par), gid,
                   (lambda par_, d: lambda e: e.tensor_add(
                       s.xn[:, d * T:(d + 1) * T], s.pA[par_],
                       s.xn[:, d * T:(d + 1) * T]))(par, dc))
        tv_y = s.c("tv")
        tp_wo = s.c("tp")

        fin2 = s.ln_stats(s.xn, reg[:, 0:8192],
                          [("tv", tv_y), ("tp", tp_wo)])

        mixed3 = s.wkvmix.rearrange("p (a b) -> p a b", b=T)
        downB = s.ffn_down(1, tv_fuseB)
        g2 = None
        tv2 = ta2 = None
        tv_q1_done = s.tv_q_done
        for i in range(16):
            next(downB)
            if i == 1:
                tv2, ta2 = fin2(extra_copy=[("tv", tv_q1_done)])
                g2 = s.dve_ln_mix(
                    [(mixed3, "cmm")],
                    "n2w", "n2b", s.h2halo[sq], tv2, ta2, tp_wo)
            if g2 is not None:
                s.drip(g2, 2)
        while s.drip(g2, 64):
            pass
        tv_mixed = s.c("tv")
        tv_h2h = s.tv_h_done
        if ci < last:
            s.x_load(ci + 1, waits=[("tv", tv_mixed)])
        tp_downB = s.c("tp")

        rcm = s.fuse[:, 0:8192]
        jc = None
        ringc = None
        for dc in range(NDC):
            par = dc % 2
            if dc % 2 == 0:
                jc = s.slab_dma("wcr", dc // 2)
                ringc = s.ring[jc["slot"]]
            fd = dc % 2
            pairs = [(ringc[:, kc * 256 + fd * 128: kc * 256 + fd * 128 + 128],
                      s.wkvmix[:, kc * T:(kc + 1) * T]) for kc in range(16)]
            gid = s.mm_group(("A", par), s.pA[par], pairs, [jc],
                             extra_waits=[("tv", tv_mixed)])
            s.evac("ta", ("A", par), gid,
                   (lambda par_, d: lambda e: e.activation(
                       rcm[:, d * T:(d + 1) * T], s.pA[par_],
                       AF.Sigmoid))(par, dc))

        tv_accy = s.op("vector", lambda e: e.tensor_add(acc3, acc3, xn3),
                       inc="tv")

        jk2 = None
        ringk2 = None
        for f in range(48):
            par = f % 2
            if f % 2 == 0:
                jk2 = s.slab_dma("wck", f // 2)
                ringk2 = s.ring[jk2["slot"]]
            fd = f % 2
            pairs = [(ringk2[:, kc * 256 + fd * 128: kc * 256 + fd * 128 + 128],
                      s.wkvmix[:, kc * T:(kc + 1) * T]) for kc in range(16)]
            gid = s.mm_group(("A", par), s.pA[par], pairs, [jk2],
                             extra_waits=[("tv", tv_mixed)])
            s.evac("ta", ("A", par), gid,
                   (lambda par_, f_: lambda e: e.activation(
                       reg[:, f_ * T:(f_ + 1) * T], s.pA[par_],
                       AF.Square))(par, f),
                   extra_waits=[("tv", max(tv_h2h, tv_accy))]
                   if f < 2 else ())
        ta_kcm = s.c("ta")
        tp_wck = s.c("tp")

        fin_next = None
        for dout in range(NDC):
            par = dout % 2
            jvl = s.slab_dma("wcv_lo", dout, used_elems=3072)
            jvh = s.slab_dma("wcv_hi", dout, used_elems=3072)
            rvl, rvh = s.ring[jvl["slot"]], s.ring[jvh["slot"]]
            pairs = [((rvl[:, kc * 128:(kc + 1) * 128] if kc < 24
                       else rvh[:, (kc - 24) * 128:(kc - 23) * 128]),
                      reg[:, kc * T:(kc + 1) * T]) for kc in range(48)]
            gid = s.mm_group(("A", par), s.pA[par], pairs, [jvl, jvh],
                             extra_waits=[("ta", ta_kcm)])
            s.op("vector",
                 (lambda par_, d: lambda e: e.tensor_mul(
                     s.t3, s.pA[par_], rcm[:, d * T:(d + 1) * T]))(par, dout),
                 waits=[("tp", gid)])
            tvv = s.op("vector",
                       (lambda d: lambda e: e.tensor_add(
                           s.acc[:, d * T:(d + 1) * T],
                           s.acc[:, d * T:(d + 1) * T], s.t3))(dout),
                       inc="tv")
            s.bank_free[("A", par)] = ("tv", tvv)
            if ci < last:
                if dout == 0:
                    fin_next = s.ln_stats(
                        s.xh[:, 0:8192], s.wkvmix[:, 0:8192],
                        [("qx", 16 * (ci + 2)), ("tp", tp_wck)],
                        extra_stat=())
                elif dout == 4:
                    tv1n, ta1n = fin_next(extra_copy=[("tv", s.tv_q_done)])
                    x3n = s.xh[:, 0:8192].rearrange("p (a b) -> p a b", b=T)

                    def bc3(t):
                        return t.unsqueeze(1).broadcast_to((128, NDC, T))
                    s.op("vector",
                         lambda e: e.tensor_mul(xn3, x3n, bc3(s.rB)),
                         waits=())
                    s.tv_xn_pending = s.op(
                        "vector",
                        lambda e: e.tensor_sub(xn3, xn3, bc3(s.mrB)),
                        waits=[("ta", ta1n)], inc="tv")
        tv_done = s.c("tv")

        s.op("sync",
             (lambda c_, sq_: lambda e: e.dma_start(
                 out=s.out5[sq_, c_], in_=s.acc))(c, sq),
             waits=[("tv", tv_done)], inc16="qo")
        s.nout += 1

    def build(self):
        self.setup()
        for ci in range(self.nchunk * self.nseq):
            self.chunk(ci)
        qo = self.c("qo")
        self.op("sync", lambda e: e.nop(), waits=[("qo", qo)])
        return self.finalize()


def host_inputs(inputs, nchunk=4, nseq=NSEQ, cores=NCORE):
    x = np.asarray(inputs["L0_attn_x"])
    seqlen = nchunk * T

    wcv = np.asarray(inputs["L0_attn_Wcv"])
    wd = np.asarray(inputs["L0_ffn_w_down"])
    wb = {
        "wr": _block_w(np.asarray(inputs["L0_attn_Wr"]), 256),
        "wk": _block_w(np.asarray(inputs["L0_attn_Wk"]), 256),
        "wv": _block_w(np.asarray(inputs["L0_attn_Wv"]), 256),
        "wo": _block_w(np.asarray(inputs["L0_attn_Wo"]), 256),
        "wcr": _block_w(np.asarray(inputs["L0_attn_Wcr"]), 256),
        "wck": _block_w(np.asarray(inputs["L0_attn_Wck"]), 256),
        "wcv_lo": _block_w(wcv[:3072], 128),
        "wcv_hi": _block_w(wcv[3072:], 128),
        "wg": _block_w(np.asarray(inputs["L0_ffn_w_gate"]), 256),
        "wu": _block_w(np.asarray(inputs["L0_ffn_w_up"]), 256),
        "wd_lo": _block_w(wd[:4096], 128),
        "wd_hi": _block_w(wd[4096:], 128),
    }
    consts = np.concatenate([
        _vec128(inputs["L0_attn_norm1_w"]), _vec128(inputs["L0_attn_norm1_b"]),
        _vec128(inputs["L0_attn_norm2_w"]), _vec128(inputs["L0_attn_norm2_b"]),
        _vec128(inputs["L0_attn_time_decay"]),
        _vec128(inputs["L0_attn_time_faaaa"]),
        _vec128(inputs["L0_attn_time_mix_r"]),
        _vec128(inputs["L0_attn_time_mix_k"]),
        _vec128(inputs["L0_attn_time_mix_v"]),
        _vec128(inputs["L0_attn_cm_mix"]),
        _vec128(inputs["L0_ffn_s_down"]),
        _vec128(inputs["L0_ffn_s_gate"]), _vec128(inputs["L0_ffn_s_up"]),
    ], axis=1).astype(np.float32)
    consts = np.ascontiguousarray(consts)
    onesb = np.ones((128, 1), dtype=bfdt)
    onesr = np.ones((1, 128), dtype=np.float32)

    in_maps = []
    for core in range(cores):
        xs = x[core * nseq:(core + 1) * nseq, :seqlen]
        x5 = np.ascontiguousarray(
            xs.reshape(nseq, nchunk, T, NDC, 128).transpose(0, 1, 4, 3, 2)
            .reshape(nseq, nchunk, 128, NDC * T))
        m = {"x5": x5, "consts": consts, "onesb": onesb, "onesr": onesr}
        m.update(wb)
        in_maps.append(m)
    return in_maps


def host_gather(results, nchunk=4, nseq=NSEQ):
    outs = []
    for r in results:
        o = np.asarray(r["out5"])
        o = o.reshape(nseq, nchunk, 128, NDC, T).transpose(0, 1, 4, 3, 2)
        outs.append(o.reshape(nseq, nchunk * T, D))
    return np.concatenate(outs, axis=0)


def kernel(**inputs):
    b = Builder(nchunk=4, nseq=NSEQ)
    nc = b.build()
    in_maps = host_inputs(inputs)
    res = run_bass_kernel_spmd(nc, in_maps, core_ids=list(range(NCORE)))
    return host_gather(res.results).astype(np.float32)


# version 10
# speedup vs baseline: 1.1146x; 1.1146x over previous
import numpy as np
import ml_dtypes
import concourse.bass as bass
import concourse.mybir as mybir
from concourse.bass_utils import run_bass_kernel_spmd

F32 = mybir.dt.float32
BF16 = mybir.dt.bfloat16
I8 = mybir.dt.int8
AF = mybir.ActivationFunctionType
ALU = mybir.AluOpType

D = 2048
NDC = D // 128
T = 512
NCORE = 8
NSEQ = 2
EPS = 1e-5
RING = 3
SLOT_F = 4096

bfdt = ml_dtypes.bfloat16

_CONST_COLS = {}
_c = 0
for _n, _w in [("n1w", 16), ("n1b", 16), ("n2w", 16), ("n2b", 16),
               ("td", 16), ("tf", 16), ("tmr", 16), ("tmk", 16),
               ("tmv", 16), ("cmm", 16), ("sdown", 16),
               ("sgate", 64), ("sup", 64)]:
    _CONST_COLS[_n] = (_c, _w)
    _c += _w
NCONST = _c


def _vec128(v):
    v = np.asarray(v).reshape(-1)
    return np.ascontiguousarray(v.reshape(-1, 128).T)


def _block_w(w, wm):
    K, M = w.shape
    nk, nm = K // 128, M // wm
    return np.ascontiguousarray(
        w.reshape(nk, 128, nm, wm).transpose(2, 1, 0, 3).reshape(nm, 128, nk * wm))


class _Op:
    __slots__ = ("eng", "emit", "waits", "incs", "nodrain")

    def __init__(self, eng, emit, waits, incs, nodrain=False):
        self.eng, self.emit, self.waits, self.incs = eng, emit, waits, incs
        self.nodrain = nodrain


class Builder:
    def __init__(self, nchunk=4, nseq=NSEQ, debug=False):
        self.nchunk = nchunk
        self.nseq = nseq
        self.debug = debug
        nc = bass.Bass("TRN2", target_bir_lowering=False)
        self.nc = nc
        self.ops = []
        self.cnt = {}
        self.sems = {}
        semnames = ["qx", "qc", "qo", "qs", "tp", "tv", "ta", "tg"] + [
            f"qw{i}" for i in range(RING)]
        for s in semnames:
            self.sems[s] = nc.alloc_semaphore(s)
            self.cnt[s] = 0

        dt = nc.dram_tensor
        self.x5 = dt("x5", [nseq, nchunk, 128, NDC * T], F32, kind="ExternalInput").ap()
        self.out5 = dt("out5", [nseq, nchunk, 128, NDC * T], F32,
                       kind="ExternalOutput").ap()
        self.wdram = {}
        for name, shape, dtype in [
            ("wr", [8, 128, 4096], F32), ("wk", [8, 128, 4096], F32),
            ("wv", [8, 128, 4096], F32), ("wo", [8, 128, 4096], F32),
            ("wcr", [8, 128, 4096], F32), ("wck", [24, 128, 4096], F32),
            ("wcv_lo", [16, 128, 3072], F32), ("wcv_hi", [16, 128, 3072], F32),
            ("wg", [32, 128, 4096], I8), ("wu", [32, 128, 4096], I8),
            ("wd_lo", [16, 128, 4096], I8), ("wd_hi", [16, 128, 4096], I8),
        ]:
            self.wdram[name] = dt(name, shape, dtype, kind="ExternalInput").ap()
        self.scratch = {}
        for name in ("wr", "wk", "wv", "wo", "wcr", "wck",
                     "wcv_lo", "wcv_hi"):
            shape = list(self.wdram[name].shape)
            self.scratch[name] = dt("sc_" + name, shape, BF16,
                                    kind="Internal").ap()
        self.constsd = dt("consts", [128, NCONST], F32, kind="ExternalInput").ap()
        self.onesbd = dt("onesb", [128, 1], BF16, kind="ExternalInput").ap()
        self.onesrd = dt("onesr", [1, 128], F32, kind="ExternalInput").ap()

        al = nc.alloc_sbuf_tensor
        self.ring = [al(f"ring{i}", [128, SLOT_F], BF16).ap() for i in range(RING)]
        self.xh = al("xh", [128, 8192], BF16).ap()
        self.xn = al("xn", [128, 8192], BF16).ap()
        self.fuse = al("fuse", [128, 16384], BF16).ap()
        self.region = al("region", [128, 24576], BF16).ap()
        self.wkvmix = al("wkvmix", [128, 8192], BF16).ap()
        self.acc = al("acc", [128, 8192], F32).ap()
        self.consts = al("consts_t", [128, NCONST], F32).ap()
        self.onesb = al("onesb_t", [128, 1], BF16).ap()
        self.onesr = al("onesr_t", [1, 128], F32).ap()
        self.ed = al("ed_t", [128, 16], F32).ap()
        self.wdec = al("wdec_t", [128, 16], F32).ap()
        self.efa = al("efa_t", [128, 16], F32).ap()
        self.carry = [al(f"carry{s}", [128, 16], F32).ap() for s in range(nseq)]
        self.h1halo = [al(f"h1halo{s}", [128, 16], BF16).ap() for s in range(nseq)]
        self.h2halo = [al(f"h2halo{s}", [128, 16], BF16).ap() for s in range(nseq)]
        self.rB = al("rB", [128, T], F32).ap()
        self.mrB = al("mrB", [128, T], F32).ap()
        self.vecs = al("vecs", [1, T], F32).ap()
        self.rsig = [al(f"rsig{i}", [128, T], F32).ap() for i in range(2)]
        self.kt = [al(f"kt{i}", [128, T], F32).ap() for i in range(2)]
        self.kvt = al("kvt", [128, T], F32).ap()
        self.gatet = [al(f"gatet{i}", [128, T], BF16).ap() for i in range(2)]
        self.ut = al("ut_t", [128, 16], F32).ap()
        self.d0t = al("d0t", [128, 16], BF16).ap()
        self.t3 = al("t3", [128, T], F32).ap()
        self.epsc = al("epsc", [1, 1], F32).ap()

        ps = nc.alloc_psum_tensor
        self.pA = [ps(f"pA{i}", [128, T], F32).ap() for i in range(2)]
        self.pB = [ps(f"pB{i}", [128, T], F32).ap() for i in range(2)]
        self.pD = [ps(f"pD{i}", [128, T], F32).ap() for i in range(2)]
        self.pS0 = ps("pS0", [1, T], F32).ap()
        self.pS1 = ps("pS1", [1, T], F32).ap()

        self.bank_free = {}
        self.slab_idx = 0
        self.slab_consumed = {}
        self.slab_store = {}
        self.cur_ci = 0
        self.scratch_wait_done = False
        self.nx_dma = 0
        self.nout = 0
        self.dbg_out = {}
        self.ta_setup = 0
        self.tv_xn_pending = 0
        self.pending_gu = None

    def op(self, eng, emit, waits=(), inc=None, inc16=None, nodrain=False):
        wl = []
        for (s, v) in waits:
            assert v <= self.cnt[s], f"wait on future {s}: {v} > {self.cnt[s]}"
            if v > 0:
                wl.append((s, v))
        incs = []
        if inc:
            self.cnt[inc] += 1
            incs.append((inc, 1))
        if inc16:
            self.cnt[inc16] += 16
            incs.append((inc16, 16))
        self.ops.append(_Op(eng, emit, wl, incs, nodrain=nodrain))
        return self.cnt[incs[0][0]] if incs else None

    def gop(self, emit, waits=()):
        return self.op("gpsimd", emit, waits=waits, inc="tg", nodrain=True)

    def c(self, s):
        return self.cnt[s]

    def cv(self, name, i=None, n=1):
        c0, w = _CONST_COLS[name]
        if i is None:
            return self.consts[:, c0:c0 + w]
        return self.consts[:, c0 + i:c0 + i + n]

    def finalize(self):
        nc = self.nc
        eng_ops = {"sync": [], "scalar": [], "vector": [], "tensor": [], "gpsimd": []}
        for o in self.ops:
            eng_ops[o.eng].append(o)

        def body(eng, name):
            do_drain = name in ("vector", "scalar")
            guaranteed = {}
            for o in eng_ops[name]:
                for (s, v) in o.waits:
                    if guaranteed.get(s, 0) >= v:
                        continue
                    eng.wait_ge(self.sems[s], v)
                    guaranteed[s] = v
                inst = o.emit(eng)
                for (s, v) in o.incs:
                    inst.then_inc(self.sems[s], v)
                if do_drain and not o.nodrain:
                    eng.drain()

        with nc.Block() as blk:
            blk.sync(lambda e: body(e, "sync"))
            blk.scalar(lambda e: body(e, "scalar"))
            blk.vector(lambda e: body(e, "vector"))
            blk.tensor(lambda e: body(e, "tensor"))
            blk.gpsimd(lambda e: body(e, "gpsimd"))
        return nc

    def setup(self):
        s = self
        s.op("sync", lambda e: e.dma_start(out=s.consts, in_=s.constsd), inc16="qc")
        s.op("sync", lambda e: e.dma_start(out=s.onesb, in_=s.onesbd), inc16="qc")
        s.op("sync", lambda e: e.dma_start(out=s.onesr, in_=s.onesrd), inc16="qc")
        qc = s.c("qc")
        s.x_load(0)
        s.op("tensor", lambda e: e.nop(), waits=[("qc", qc)])
        s.op("scalar", lambda e: e.activation(s.ed, s.cv("td"), AF.Exp),
             waits=[("qc", qc)], inc="ta")
        s.op("scalar", lambda e: e.activation(s.wdec, s.ed, AF.Exp, scale=-1.0),
             inc="ta")
        s.op("scalar", lambda e: e.activation(s.efa, s.cv("tf"), AF.Exp), inc="ta")
        s.ta_setup = s.c("ta")
        for t in s.carry + s.h1halo + s.h2halo:
            s.op("vector", (lambda t_: lambda e: e.memset(t_, 0.0))(t))
        s.op("vector", lambda e: e.memset(s.epsc, EPS))
        s.op("vector", lambda e: e.engine_nop(), inc="tv")

    def x_load(self, ci, waits=()):
        s = self
        c, sq = divmod(ci, s.nseq)
        s.op("gpsimd",
             (lambda sr: lambda e: e.dma_start(out=s.xh, in_=sr))(s.x5[sq, c]),
             waits=list(waits), inc16="qx", nodrain=True)
        s.nx_dma += 1

    def slab_dma(self, wname, m, used_elems=SLOT_F):
        s = self
        j = s.slab_idx
        s.slab_idx += 1
        slot = j % RING
        waits = []
        if j - RING in s.slab_consumed:
            waits.append(("tp", s.slab_consumed[j - RING]))
        if j - RING in s.slab_store:
            waits.append(("qs", s.slab_store[j - RING]))
        dst = s.ring[slot][:, 0:used_elems]
        sem = f"qw{slot}"
        fam32 = wname in s.scratch
        if fam32 and s.cur_ci > 0:
            if not s.scratch_wait_done:
                waits.append(("qs", s.c("qs")))
                s.scratch_wait_done = True
            src = s.scratch[wname][m]
            tgt = s.op("sync",
                       (lambda d2, so: lambda e: e.dma_start(out=d2, in_=so))(
                           dst, src),
                       waits=waits, inc16=sem)
        else:
            src = s.wdram[wname][m]
            tgt = s.op("gpsimd",
                       (lambda d2, so: lambda e: e.dma_start(out=d2, in_=so))(
                           dst, src),
                       waits=waits, inc16=sem, nodrain=True)
            if fam32:
                stgt = s.op("sync",
                            (lambda d2, so: lambda e: e.dma_start(
                                out=so, in_=d2))(dst, s.scratch[wname][m]),
                            waits=[(sem, tgt)], inc16="qs")
                s.slab_store[j] = stgt
        return {"j": j, "slot": slot, "wait": (sem, tgt)}

    def mm_group(self, bank_key, bank_ap, pairs, slabs, extra_waits=(),
                 open_start=True, close_stop=True, mark_slabs=None):
        s = self
        waits = list(extra_waits)
        for tok in slabs:
            waits.append(tok["wait"])
        if open_start:
            bf = s.bank_free.get(bank_key)
            if bf:
                waits.append(bf)
        n = len(pairs)
        gid = None
        for i, (lh, rh) in enumerate(pairs):
            st = open_start and (i == 0)
            sp = close_stop and (i == n - 1)
            gid = s.op("tensor",
                       (lambda l, r, st_, sp_, b: lambda e: e.matmul(
                           b, l, r, start=st_, stop=sp_))(lh, rh, st, sp, bank_ap),
                       waits=waits if i == 0 else (),
                       inc="tp" if sp else None)
        if close_stop:
            for tok in (slabs if mark_slabs is None else mark_slabs):
                s.slab_consumed[tok["j"]] = s.c("tp")
        return gid

    def evac(self, eng_sem, bank_key, gid, emit, extra_waits=()):
        s = self
        eng = {"tv": "vector", "ta": "scalar"}[eng_sem]
        waits = [("tp", gid)] + list(extra_waits)
        cnt = s.op(eng, emit, waits=waits, inc=eng_sem)
        s.bank_free[bank_key] = (eng_sem, cnt)
        return cnt

    def ln_stats(self, src_bf, sq_dst, ready, extra_stat=()):
        s = self
        x3 = src_bf.rearrange("p (a b) -> p a b", b=T)
        sq3 = sq_dst.rearrange("p (a b) -> p a b", b=T)
        tvsq = s.op("vector", lambda e: e.tensor_mul(sq3, x3, x3),
                    waits=list(ready), inc="tv")
        w0 = list(ready)
        if "S0" in s.bank_free:
            w0.append(s.bank_free["S0"])
        for dc in range(NDC):
            s.op("tensor",
                 (lambda d: lambda e: e.matmul(
                     s.pS0, s.onesb, src_bf[:, d * T:(d + 1) * T],
                     start=(d == 0), stop=(d == NDC - 1)))(dc),
                 waits=w0 if dc == 0 else ())
        w1 = [("tv", tvsq)]
        if "S1" in s.bank_free:
            w1.append(s.bank_free["S1"])
        gid = None
        for dc in range(NDC):
            gid = s.op("tensor",
                       (lambda d: lambda e: e.matmul(
                           s.pS1, s.onesb, sq_dst[:, d * T:(d + 1) * T],
                           start=(d == 0), stop=(d == NDC - 1)))(dc),
                       waits=w1 if dc == 0 else (),
                       inc="tp" if dc == NDC - 1 else None)
        vm = s.rB[0:1, :]
        vt = s.mrB[0:1, :]
        vmr = s.vecs
        s.op("vector", lambda e: e.tensor_scalar_mul(vm, s.pS0, 1.0 / D),
             waits=[("tp", gid)] + list(extra_stat))
        s.op("vector", lambda e: e.tensor_mul(vmr, vm, vm))
        tvv = s.op("vector",
                   lambda e: e.scalar_tensor_tensor(
                       vt, s.pS1, 1.0 / D, vmr, ALU.mult, ALU.subtract),
                   inc="tv")
        s.bank_free["S0"] = ("tv", tvv)
        s.bank_free["S1"] = ("tv", tvv)
        tav = s.op("scalar", lambda e: e.activation(vt, vt, AF.Sqrt, bias=s.epsc),
                   waits=[("tv", tvv)], inc="ta")
        s.op("vector", lambda e: e.reciprocal(vt, vt), waits=[("ta", tav)])
        tvr = s.op("vector", lambda e: e.tensor_mul(vmr, vm, vt), inc="tv")

        def finish(extra_copy=()):
            bc_waits = [("tv", tvr)]
            if ("D", 0) in s.bank_free:
                bc_waits.append(s.bank_free[("D", 0)])
            gid0 = s.op("tensor",
                        lambda e: e.matmul(s.pD[0], s.onesr, vt, start=True,
                                           stop=True),
                        waits=bc_waits, inc="tp")
            bc1_waits = []
            if ("D", 1) in s.bank_free:
                bc1_waits.append(s.bank_free[("D", 1)])
            gid1 = s.op("tensor",
                        lambda e: e.matmul(s.pD[1], s.onesr, vmr, start=True,
                                           stop=True),
                        waits=bc1_waits, inc="tp")
            tv1 = s.op("vector", lambda e: e.tensor_copy(s.rB, s.pD[0]),
                       waits=[("tp", gid0)], inc="tv")
            ta1 = s.op("scalar", lambda e: e.activation(s.mrB, s.pD[1], AF.Copy),
                       waits=[("tp", gid1)] + list(extra_copy), inc="ta")
            s.bank_free[("D", 0)] = ("tv", tv1)
            s.bank_free[("D", 1)] = ("ta", ta1)
            return tv1, ta1
        return finish

    def emit_preln(self, ci, ready_tp=None, extra_stat=(), apply_wait_tg=0):
        s = self
        ready = [("qx", 16 * (ci + 1))]
        if ci == 0:
            ready.append(("qc", s.c("qc")))
        if ready_tp is not None:
            ready.append(ready_tp)
        x_bf = s.xh[:, 0:8192]
        fin = s.ln_stats(x_bf, s.wkvmix[:, 0:8192], ready, extra_stat=extra_stat)
        tv1, ta1 = fin(extra_copy=list(extra_stat))
        x3 = x_bf.rearrange("p (a b) -> p a b", b=T)
        xn3 = s.xn.rearrange("p (a b) -> p a b", b=T)

        def bc3(t):
            return t.unsqueeze(1).broadcast_to((128, NDC, T))
        aw = []
        if apply_wait_tg:
            aw.append(("tg", apply_wait_tg))
        s.op("vector", lambda e: e.tensor_mul(xn3, x3, bc3(s.rB)), waits=aw)
        s.tv_xn_pending = s.op("vector",
                               lambda e: e.tensor_sub(xn3, xn3, bc3(s.mrB)),
                               waits=[("ta", ta1)], inc="tv")

    def dve_ln_mix(self, tm_items, nw, nb, halo, tv1, ta1, tp_guard):
        s = self
        q3 = s.region[:, 0:8192].rearrange("p (a b) -> p a b", b=T)
        xn3 = s.xn.rearrange("p (a b) -> p a b", b=T)
        xh3 = s.xh.rearrange("p (a b) -> p a b", b=T)

        def bc3(t):
            return t.unsqueeze(1).broadcast_to((128, NDC, T))

        def bci(t16, n=T):
            return t16.unsqueeze(2).broadcast_to((128, NDC, n))
        w0 = [("ta", ta1)]
        if tp_guard:
            w0.append(("tp", tp_guard))
        s.op("vector", lambda e: e.tensor_mul(q3, xn3, bc3(s.rB)), waits=w0)
        yield
        s.tv_q_done = s.op("vector",
                           lambda e: e.tensor_sub(q3, q3, bc3(s.mrB)),
                           inc="tv")
        yield
        s.op("vector", lambda e: e.tensor_mul(xh3, q3, bci(s.cv(nw))))
        yield
        s.tv_h_done = s.op("vector",
                           lambda e: e.tensor_add(xh3, xh3, bci(s.cv(nb))),
                           inc="tv")
        yield
        s.op("vector", lambda e: e.tensor_sub(s.d0t, xh3[:, :, 0], halo))
        yield
        first3 = tm_items[0][0]
        s.op("vector", lambda e: e.tensor_sub(
            first3[:, :, 1:512], xh3[:, :, 1:512], xh3[:, :, 0:511]))
        yield
        for (dst3_, tmkey) in tm_items:
            c0 = dst3_[:, :, 0]
            s.op("vector", (lambda c0_, tk: lambda e: e.tensor_mul(
                c0_, s.d0t, s.cv(tk)))(c0, tmkey))
            yield
            s.op("vector", (lambda c0_: lambda e: e.tensor_add(
                c0_, c0_, halo))(c0))
            yield
        s.op("vector", lambda e: e.tensor_copy(halo, xh3[:, :, 511]))
        yield
        for (dst3_, tmkey) in tm_items[1:]:
            s.op("vector", (lambda ds, tk: lambda e: e.tensor_mul(
                ds[:, :, 1:512], first3[:, :, 1:512],
                bci(s.cv(tk), 511)))(dst3_, tmkey))
            yield
        tk0 = tm_items[0][1]
        s.op("vector", (lambda tk: lambda e: e.tensor_mul(
            first3[:, :, 1:512], first3[:, :, 1:512],
            bci(s.cv(tk), 511)))(tk0))
        yield
        for (dst3_, tmkey) in tm_items:
            s.op("vector", (lambda ds: lambda e: e.tensor_add(
                ds[:, :, 1:512], ds[:, :, 1:512], xh3[:, :, 0:511]))(dst3_),
                inc="tv")
            yield

    @staticmethod
    def drip(gen, n):
        for _ in range(n):
            try:
                next(gen)
            except StopIteration:
                return False
        return True

    def ffn_gu(self, f_range, tv_xn, state):
        s = self
        for f in f_range:
            par = f % 2
            if f % 2 == 0:
                if f == 0 and s.pending_gu is not None:
                    state["jg"], state["ju"] = s.pending_gu
                    s.pending_gu = None
                else:
                    state["jg"] = s.slab_dma("wg", f // 2)
                    state["ju"] = s.slab_dma("wu", f // 2)
            jg, ju = state["jg"], state["ju"]
            ringg, ringu = s.ring[jg["slot"]], s.ring[ju["slot"]]
            fd = f % 2
            pairsg = [(ringg[:, kc * 256 + fd * 128: kc * 256 + fd * 128 + 128],
                       s.xn[:, kc * T:(kc + 1) * T]) for kc in range(16)]
            gidg = s.mm_group(("A", par), s.pA[par], pairsg, [jg],
                              extra_waits=[("tv", tv_xn)])
            ta_g = s.evac("ta", ("A", par), gidg,
                          (lambda par_, f_: lambda e: e.activation(
                              s.gatet[par_], s.pA[par_], AF.Silu,
                              scale=s.cv("sgate", f_)))(par, f),
                          extra_waits=([("tv", state["gate_free"][par])]
                                       if par in state["gate_free"] else ()))
            pairsu = [(ringu[:, kc * 256 + fd * 128: kc * 256 + fd * 128 + 128],
                       s.xn[:, kc * T:(kc + 1) * T]) for kc in range(16)]
            gidu = s.mm_group(("B", par), s.pB[par], pairsu, [ju])
            fs = s.fuse[:, (f % 32) * T:((f % 32) + 1) * T]
            tvf = s.evac("tv", ("B", par), gidu,
                         (lambda par_, f_, fs_: lambda e: e.scalar_tensor_tensor(
                             fs_, s.pB[par_], s.cv("sup", f_),
                             s.gatet[par_], ALU.mult, ALU.mult))(par, f, fs),
                         extra_waits=[("ta", ta_g)])
            state["gate_free"][par] = tvf

    def ffn_down(self, half, tv_fuse, first_extra=()):
        s = self
        wname = "wd_lo" if half == 0 else "wd_hi"
        for dout in range(16):
            par = dout % 2
            jl = s.slab_dma(wname, dout)
            rl = s.ring[jl["slot"]]
            pairs = [(rl[:, kc * 128:(kc + 1) * 128],
                      s.fuse[:, kc * T:(kc + 1) * T]) for kc in range(32)]
            gid = s.mm_group(("A", par), s.pA[par], pairs, [jl],
                             extra_waits=[("tv", tv_fuse)])
            ew = list(first_extra) if dout == 0 else []
            if half == 0:
                x_bf = s.xh[:, 0:8192]
                emit = (lambda par_, d_: lambda e: e.scalar_tensor_tensor(
                    s.acc[:, d_ * T:(d_ + 1) * T], s.pA[par_],
                    s.cv("sdown", d_), x_bf[:, d_ * T:(d_ + 1) * T],
                    ALU.mult, ALU.add))(par, dout)
            else:
                emit = (lambda par_, d_: lambda e: e.scalar_tensor_tensor(
                    s.acc[:, d_ * T:(d_ + 1) * T], s.pA[par_],
                    s.cv("sdown", d_), s.acc[:, d_ * T:(d_ + 1) * T],
                    ALU.mult, ALU.add))(par, dout)
            s.evac("tv", ("A", par), gid, emit, extra_waits=ew)
            yield

    def chunk(self, ci):
        s = self
        s.cur_ci = ci
        c, sq = divmod(ci, s.nseq)
        xn3 = s.xn.rearrange("p (a b) -> p a b", b=T)
        acc3 = s.acc.rearrange("p (a b) -> p a b", b=T)
        xh3 = s.xh.rearrange("p (a b) -> p a b", b=T)
        reg = s.region
        last = s.nchunk * s.nseq - 1

        if ci == 0:
            s.emit_preln(0)
        tv_xn = s.tv_xn_pending
        tp_chunk_start = s.c("tp")

        gu_state = {"gate_free": {}, "jg": None, "ju": None}
        s.ffn_gu(range(0, 32), tv_xn, gu_state)
        tv_fuseA = s.c("tv")

        fin1 = s.ln_stats(s.xn, s.wkvmix[:, 0:8192],
                          [("tv", tv_xn), ("tp", tp_chunk_start)])

        ew0 = [("qo", 16 * s.nout)] if s.nout > 0 else []
        downA = s.ffn_down(0, tv_fuseA, first_extra=ew0)
        for _ in range(16):
            next(downA)
        tp_downA = s.c("tp")
        tv_downA = s.c("tv")

        mixr3 = reg[:, 0:8192].rearrange("p (a b) -> p a b", b=T)
        mixk3 = reg[:, 8192:16384].rearrange("p (a b) -> p a b", b=T)
        mixv3 = reg[:, 16384:24576].rearrange("p (a b) -> p a b", b=T)
        g1 = None
        tv1 = ta1 = None
        for i, f in enumerate(range(32, 64)):
            s.ffn_gu([f], tv_xn, gu_state)
            if i == 1:
                tv1, ta1 = fin1()
                g1 = s.dve_ln_mix(
                    [(mixr3, "tmr"), (mixk3, "tmk"), (mixv3, "tmv")],
                    "n1w", "n1b", s.h1halo[sq], tv1, ta1, tp_chunk_start)
            if g1 is not None:
                s.drip(g1, 1)
        while s.drip(g1, 64):
            pass
        tv_mix = s.c("tv")
        tv_fuseB = max(gu_state["gate_free"].values())

        mixdst = {"r": reg[:, 0:8192], "k": reg[:, 8192:16384],
                  "v": reg[:, 16384:24576]}
        ring_rkv = {}
        rsig_free = {}
        kt_free = {}
        for dc in range(NDC):
            par = dc % 2
            if dc % 2 == 0:
                q = dc // 2
                for m in ("r", "k", "v"):
                    tok = s.slab_dma("w" + m, q)
                    ring_rkv[m] = (tok, s.ring[tok["slot"]])
            fd = dc % 2

            def lh(rt, kc):
                return rt[:, kc * 256 + fd * 128: kc * 256 + fd * 128 + 128]

            jr, rr = ring_rkv["r"]
            gidr = s.mm_group(("A", par), s.pA[par],
                              [(lh(rr, kc), mixdst["r"][:, kc * T:(kc + 1) * T])
                               for kc in range(16)], [jr],
                              extra_waits=[("tv", tv_mix)])
            s.evac("ta", ("A", par), gidr,
                   (lambda par_: lambda e: e.activation(
                       s.rsig[par_], s.pA[par_], AF.Sigmoid))(par),
                   extra_waits=([("tv", rsig_free[par])]
                                if par in rsig_free else ()))
            jk, rk = ring_rkv["k"]
            gidk = s.mm_group(("B", par), s.pB[par],
                              [(lh(rk, kc), mixdst["k"][:, kc * T:(kc + 1) * T])
                               for kc in range(16)], [jk])
            s.evac("ta", ("B", par), gidk,
                   (lambda par_: lambda e: e.copy(s.kt[par_], s.pB[par_]))(par),
                   extra_waits=([("tv", kt_free[par])]
                                if par in kt_free else ()))
            jv, rv = ring_rkv["v"]
            gidv = s.mm_group(("D", par), s.pD[par],
                              [(lh(rv, kc), mixdst["v"][:, kc * T:(kc + 1) * T])
                               for kc in range(16)], [jv])
            kt_free[par] = s.evac(
                "tv", ("D", par), gidv,
                (lambda par_: lambda e: e.tensor_mul(
                    s.kvt, s.pD[par_], s.kt[par_]))(par),
                extra_waits=[("ta", s.c("ta"))])
            scan_waits = [("ta", s.ta_setup)] if ci == 0 and dc == 0 else ()
            s.op("vector",
                 (lambda d: lambda e: e.tensor_tensor_scan(
                     s.kvt, s.wdec[:, d:d + 1].broadcast_to((128, T)), s.kvt,
                     s.carry[sq][:, d:d + 1], ALU.mult, ALU.add))(dc),
                 waits=scan_waits)
            s.op("vector",
                 (lambda d: lambda e: e.tensor_copy(
                     s.carry[sq][:, d:d + 1], s.kvt[:, T - 1:T]))(dc))
            wkv_dst = s.wkvmix[:, dc * T:(dc + 1) * T]
            rsig_free[par] = s.op(
                "vector",
                (lambda par_, ds: lambda e: e.tensor_mul(
                    ds, s.rsig[par_], s.kvt))(par, wkv_dst), inc="tv")
            if c == 0:
                s.op("vector",
                     (lambda d: lambda e: e.tensor_scalar_mul(
                         s.ut[:, 0:1], s.kvt[:, 0:1], s.efa[:, d:d + 1]))(dc))
                s.op("vector",
                     (lambda par_: lambda e: e.tensor_mul(
                         s.ut[:, 0:1], s.ut[:, 0:1],
                         s.rsig[par_][:, 0:1]))(par))
                rsig_free[par] = s.op(
                    "vector",
                    (lambda ds: lambda e: e.tensor_add(
                        ds[:, 0:1], ds[:, 0:1], s.ut[:, 0:1]))(wkv_dst),
                    inc="tv")
        tv_wkv = s.op("vector", lambda e: e.engine_nop(), inc="tv")

        jo = None
        ringo = None
        for dc in range(NDC):
            par = dc % 2
            if dc % 2 == 0:
                jo = s.slab_dma("wo", dc // 2)
                ringo = s.ring[jo["slot"]]
            fd = dc % 2
            pairs = [(ringo[:, kc * 256 + fd * 128: kc * 256 + fd * 128 + 128],
                      s.wkvmix[:, kc * T:(kc + 1) * T]) for kc in range(16)]
            gid = s.mm_group(("A", par), s.pA[par], pairs, [jo],
                             extra_waits=[("tv", tv_wkv)])
            s.evac("tv", ("A", par), gid,
                   (lambda par_, d: lambda e: e.tensor_add(
                       s.xn[:, d * T:(d + 1) * T], s.pA[par_],
                       s.xn[:, d * T:(d + 1) * T]))(par, dc))
        tv_y = s.c("tv")
        tp_wo = s.c("tp")

        mixed3 = s.wkvmix.rearrange("p (a b) -> p a b", b=T)
        downB = s.ffn_down(1, tv_fuseB)
        g2 = None
        fin2 = None
        tv2 = ta2 = None
        tv_q1_done = s.tv_q_done
        for i in range(16):
            next(downB)
            if i == 0:
                fin2 = s.ln_stats(s.xn, reg[:, 0:8192],
                                  [("tv", tv_y), ("tp", tp_wo)])
            if i == 1:
                tv2, ta2 = fin2(extra_copy=[("tv", tv_q1_done)])
                g2 = s.dve_ln_mix(
                    [(mixed3, "cmm")],
                    "n2w", "n2b", s.h2halo[sq], tv2, ta2, tp_wo)
            if g2 is not None:
                s.drip(g2, 2)
        while s.drip(g2, 64):
            pass
        tv_mixed = s.c("tv")
        tv_h2h = s.tv_h_done
        if ci < last:
            s.x_load(ci + 1, waits=[("tv", tv_mixed)])
        tp_downB = s.c("tp")

        rcm = s.fuse[:, 0:8192]
        jc = None
        ringc = None
        for dc in range(NDC):
            par = dc % 2
            if dc % 2 == 0:
                jc = s.slab_dma("wcr", dc // 2)
                ringc = s.ring[jc["slot"]]
            fd = dc % 2
            pairs = [(ringc[:, kc * 256 + fd * 128: kc * 256 + fd * 128 + 128],
                      s.wkvmix[:, kc * T:(kc + 1) * T]) for kc in range(16)]
            gid = s.mm_group(("A", par), s.pA[par], pairs, [jc],
                             extra_waits=[("tv", tv_mixed)])
            s.evac("ta", ("A", par), gid,
                   (lambda par_, d: lambda e: e.activation(
                       rcm[:, d * T:(d + 1) * T], s.pA[par_],
                       AF.Sigmoid))(par, dc))

        tv_accy = s.op("vector", lambda e: e.tensor_add(acc3, acc3, xn3),
                       inc="tv")

        jk2 = None
        ringk2 = None
        for f in range(48):
            par = f % 2
            if f % 2 == 0:
                jk2 = s.slab_dma("wck", f // 2)
                ringk2 = s.ring[jk2["slot"]]
            fd = f % 2
            pairs = [(ringk2[:, kc * 256 + fd * 128: kc * 256 + fd * 128 + 128],
                      s.wkvmix[:, kc * T:(kc + 1) * T]) for kc in range(16)]
            gid = s.mm_group(("A", par), s.pA[par], pairs, [jk2],
                             extra_waits=[("tv", tv_mixed)])
            s.evac("ta", ("A", par), gid,
                   (lambda par_, f_: lambda e: e.activation(
                       reg[:, f_ * T:(f_ + 1) * T], s.pA[par_],
                       AF.Square))(par, f),
                   extra_waits=[("tv", max(tv_h2h, tv_accy))]
                   if f < 2 else ())
        ta_kcm = s.c("ta")
        tp_wck = s.c("tp")

        fin_next = None
        for dout in range(NDC):
            par = dout % 2
            jvl = s.slab_dma("wcv_lo", dout, used_elems=3072)
            rvl = s.ring[jvl["slot"]]
            pairs1 = [(rvl[:, kc * 128:(kc + 1) * 128],
                       reg[:, kc * T:(kc + 1) * T]) for kc in range(24)]
            s.mm_group(("A", par), s.pA[par], pairs1, [jvl],
                       extra_waits=[("ta", ta_kcm)],
                       close_stop=False, mark_slabs=[])
            jvh = s.slab_dma("wcv_hi", dout, used_elems=3072)
            rvh = s.ring[jvh["slot"]]
            pairs2 = [(rvh[:, kc * 128:(kc + 1) * 128],
                       reg[:, (24 + kc) * T:(25 + kc) * T]) for kc in range(24)]
            gid = s.mm_group(("A", par), s.pA[par], pairs2, [jvh],
                             open_start=False, mark_slabs=[jvl, jvh])
            s.op("vector",
                 (lambda par_, d: lambda e: e.tensor_mul(
                     s.t3, s.pA[par_], rcm[:, d * T:(d + 1) * T]))(par, dout),
                 waits=[("tp", gid)])
            tvv = s.op("vector",
                       (lambda d: lambda e: e.tensor_add(
                           s.acc[:, d * T:(d + 1) * T],
                           s.acc[:, d * T:(d + 1) * T], s.t3))(dout),
                       inc="tv")
            s.bank_free[("A", par)] = ("tv", tvv)
            if ci < last:
                if dout == 0:
                    fin_next = s.ln_stats(
                        s.xh[:, 0:8192], s.wkvmix[:, 0:8192],
                        [("qx", 16 * (ci + 2)), ("tp", tp_wck)],
                        extra_stat=())
                elif dout == 4:
                    tv1n, ta1n = fin_next(extra_copy=[("tv", s.tv_q_done)])
                    x3n = s.xh[:, 0:8192].rearrange("p (a b) -> p a b", b=T)

                    def bc3(t):
                        return t.unsqueeze(1).broadcast_to((128, NDC, T))
                    s.op("vector",
                         lambda e: e.tensor_mul(xn3, x3n, bc3(s.rB)),
                         waits=())
                    s.tv_xn_pending = s.op(
                        "vector",
                        lambda e: e.tensor_sub(xn3, xn3, bc3(s.mrB)),
                        waits=[("ta", ta1n)], inc="tv")
        tv_done = s.c("tv")

        if ci < last:
            s.pending_gu = (s.slab_dma("wg", 0), s.slab_dma("wu", 0))

        s.op("sync",
             (lambda c_, sq_: lambda e: e.dma_start(
                 out=s.out5[sq_, c_], in_=s.acc))(c, sq),
             waits=[("tv", tv_done)], inc16="qo")
        s.nout += 1

    def build(self):
        self.setup()
        for ci in range(self.nchunk * self.nseq):
            self.chunk(ci)
        qo = self.c("qo")
        self.op("sync", lambda e: e.nop(), waits=[("qo", qo)])
        return self.finalize()


def host_inputs(inputs, nchunk=4, nseq=NSEQ, cores=NCORE):
    x = np.asarray(inputs["L0_attn_x"])
    seqlen = nchunk * T

    wcv = np.asarray(inputs["L0_attn_Wcv"])
    wd = np.asarray(inputs["L0_ffn_w_down"])
    wb = {
        "wr": _block_w(np.asarray(inputs["L0_attn_Wr"]), 256),
        "wk": _block_w(np.asarray(inputs["L0_attn_Wk"]), 256),
        "wv": _block_w(np.asarray(inputs["L0_attn_Wv"]), 256),
        "wo": _block_w(np.asarray(inputs["L0_attn_Wo"]), 256),
        "wcr": _block_w(np.asarray(inputs["L0_attn_Wcr"]), 256),
        "wck": _block_w(np.asarray(inputs["L0_attn_Wck"]), 256),
        "wcv_lo": _block_w(wcv[:3072], 128),
        "wcv_hi": _block_w(wcv[3072:], 128),
        "wg": _block_w(np.asarray(inputs["L0_ffn_w_gate"]), 256),
        "wu": _block_w(np.asarray(inputs["L0_ffn_w_up"]), 256),
        "wd_lo": _block_w(wd[:4096], 128),
        "wd_hi": _block_w(wd[4096:], 128),
    }
    consts = np.concatenate([
        _vec128(inputs["L0_attn_norm1_w"]), _vec128(inputs["L0_attn_norm1_b"]),
        _vec128(inputs["L0_attn_norm2_w"]), _vec128(inputs["L0_attn_norm2_b"]),
        _vec128(inputs["L0_attn_time_decay"]),
        _vec128(inputs["L0_attn_time_faaaa"]),
        _vec128(inputs["L0_attn_time_mix_r"]),
        _vec128(inputs["L0_attn_time_mix_k"]),
        _vec128(inputs["L0_attn_time_mix_v"]),
        _vec128(inputs["L0_attn_cm_mix"]),
        _vec128(inputs["L0_ffn_s_down"]),
        _vec128(inputs["L0_ffn_s_gate"]), _vec128(inputs["L0_ffn_s_up"]),
    ], axis=1).astype(np.float32)
    consts = np.ascontiguousarray(consts)
    onesb = np.ones((128, 1), dtype=bfdt)
    onesr = np.ones((1, 128), dtype=np.float32)

    in_maps = []
    for core in range(cores):
        xs = x[core * nseq:(core + 1) * nseq, :seqlen]
        x5 = np.ascontiguousarray(
            xs.reshape(nseq, nchunk, T, NDC, 128).transpose(0, 1, 4, 3, 2)
            .reshape(nseq, nchunk, 128, NDC * T))
        m = {"x5": x5, "consts": consts, "onesb": onesb, "onesr": onesr}
        m.update(wb)
        in_maps.append(m)
    return in_maps


def host_gather(results, nchunk=4, nseq=NSEQ):
    outs = []
    for r in results:
        o = np.asarray(r["out5"])
        o = o.reshape(nseq, nchunk, 128, NDC, T).transpose(0, 1, 4, 3, 2)
        outs.append(o.reshape(nseq, nchunk * T, D))
    return np.concatenate(outs, axis=0)


def kernel(**inputs):
    b = Builder(nchunk=4, nseq=NSEQ)
    nc = b.build()
    in_maps = host_inputs(inputs)
    res = run_bass_kernel_spmd(nc, in_maps, core_ids=list(range(NCORE)))
    return host_gather(res.results).astype(np.float32)


# version 12
# speedup vs baseline: 1.1397x; 1.0225x over previous
import numpy as np
import ml_dtypes
import concourse.bass as bass
import concourse.mybir as mybir
from concourse.bass_utils import run_bass_kernel_spmd

F32 = mybir.dt.float32
BF16 = mybir.dt.bfloat16
I8 = mybir.dt.int8
AF = mybir.ActivationFunctionType
ALU = mybir.AluOpType

D = 2048
NDC = D // 128
T = 512
NCORE = 8
NSEQ = 2
EPS = 1e-5
RING = 3
SLOT_F = 4096

bfdt = ml_dtypes.bfloat16

_CONST_COLS = {}
_c = 0
for _n, _w in [("n1w", 16), ("n1b", 16), ("n2w", 16), ("n2b", 16),
               ("td", 16), ("tf", 16), ("tmr", 16), ("tmk", 16),
               ("tmv", 16), ("cmm", 16), ("sdown", 16),
               ("sgate", 64), ("sup", 64)]:
    _CONST_COLS[_n] = (_c, _w)
    _c += _w
NCONST = _c


def _vec128(v):
    v = np.asarray(v).reshape(-1)
    return np.ascontiguousarray(v.reshape(-1, 128).T)


def _block_w(w, wm):
    K, M = w.shape
    nk, nm = K // 128, M // wm
    return np.ascontiguousarray(
        w.reshape(nk, 128, nm, wm).transpose(2, 1, 0, 3).reshape(nm, 128, nk * wm))


class _Op:
    __slots__ = ("eng", "emit", "waits", "incs", "nodrain")

    def __init__(self, eng, emit, waits, incs, nodrain=False):
        self.eng, self.emit, self.waits, self.incs = eng, emit, waits, incs
        self.nodrain = nodrain


class Builder:
    def __init__(self, nchunk=4, nseq=NSEQ, debug=False):
        self.nchunk = nchunk
        self.nseq = nseq
        self.debug = debug
        nc = bass.Bass("TRN2", target_bir_lowering=False)
        self.nc = nc
        self.ops = []
        self.cnt = {}
        self.sems = {}
        semnames = ["qx", "qc", "qo", "qs", "tp", "tv", "ta", "tg"] + [
            f"qw{i}" for i in range(RING)]
        for s in semnames:
            self.sems[s] = nc.alloc_semaphore(s)
            self.cnt[s] = 0

        dt = nc.dram_tensor
        self.x5 = dt("x5", [nseq, nchunk, 128, NDC * T], F32, kind="ExternalInput").ap()
        self.out5 = dt("out5", [nseq, nchunk, 128, NDC * T], F32,
                       kind="ExternalOutput").ap()
        self.wdram = {}
        for name, shape, dtype in [
            ("wr", [8, 128, 4096], F32), ("wk", [8, 128, 4096], F32),
            ("wv", [8, 128, 4096], F32), ("wo", [8, 128, 4096], F32),
            ("wcr", [8, 128, 4096], F32), ("wck", [24, 128, 4096], F32),
            ("wcv_lo", [16, 128, 3072], F32), ("wcv_hi", [16, 128, 3072], F32),
            ("wg", [32, 128, 4096], I8), ("wu", [32, 128, 4096], I8),
            ("wd_lo", [16, 128, 4096], I8), ("wd_hi", [16, 128, 4096], I8),
        ]:
            self.wdram[name] = dt(name, shape, dtype, kind="ExternalInput").ap()
        self.scratch = {}
        for name in ("wr", "wk", "wv", "wo", "wcr", "wck",
                     "wcv_lo", "wcv_hi"):
            shape = list(self.wdram[name].shape)
            self.scratch[name] = dt("sc_" + name, shape, BF16,
                                    kind="Internal").ap()
        self.constsd = dt("consts", [128, NCONST], F32, kind="ExternalInput").ap()
        self.onesbd = dt("onesb", [128, 1], BF16, kind="ExternalInput").ap()
        self.onesrd = dt("onesr", [1, 128], F32, kind="ExternalInput").ap()

        al = nc.alloc_sbuf_tensor
        self.ring = [al(f"ring{i}", [128, SLOT_F], BF16).ap() for i in range(RING)]
        self.xh = al("xh", [128, 8192], BF16).ap()
        self.xn = al("xn", [128, 8192], BF16).ap()
        self.fuse = al("fuse", [128, 16384], BF16).ap()
        self.region = al("region", [128, 24576], BF16).ap()
        self.wkvmix = al("wkvmix", [128, 8192], BF16).ap()
        self.acc = al("acc", [128, 8192], F32).ap()
        self.consts = al("consts_t", [128, NCONST], F32).ap()
        self.onesb = al("onesb_t", [128, 1], BF16).ap()
        self.onesr = al("onesr_t", [1, 128], F32).ap()
        self.ed = al("ed_t", [128, 16], F32).ap()
        self.wdec = al("wdec_t", [128, 16], F32).ap()
        self.efa = al("efa_t", [128, 16], F32).ap()
        self.carry = [al(f"carry{s}", [128, 16], F32).ap() for s in range(nseq)]
        self.h1halo = [al(f"h1halo{s}", [128, 16], BF16).ap() for s in range(nseq)]
        self.h2halo = [al(f"h2halo{s}", [128, 16], BF16).ap() for s in range(nseq)]
        self.rB = al("rB", [128, T], F32).ap()
        self.mrB = al("mrB", [128, T], F32).ap()
        self.vecs = al("vecs", [1, T], F32).ap()
        self.rsig = [al(f"rsig{i}", [128, T], F32).ap() for i in range(2)]
        self.kt = [al(f"kt{i}", [128, T], F32).ap() for i in range(2)]
        self.kvt = al("kvt", [128, T], F32).ap()
        self.gatet = [al(f"gatet{i}", [128, T], BF16).ap() for i in range(2)]
        self.ut = al("ut_t", [128, 16], F32).ap()
        self.d0t = al("d0t", [128, 16], BF16).ap()
        self.t3 = al("t3", [128, T], F32).ap()
        self.epsc = al("epsc", [1, 1], F32).ap()

        ps = nc.alloc_psum_tensor
        self.pA = [ps(f"pA{i}", [128, T], F32).ap() for i in range(2)]
        self.pB = [ps(f"pB{i}", [128, T], F32).ap() for i in range(2)]
        self.pD = [ps(f"pD{i}", [128, T], F32).ap() for i in range(2)]
        self.pS0 = ps("pS0", [1, T], F32).ap()
        self.pS1 = ps("pS1", [1, T], F32).ap()

        self.bank_free = {}
        self.slab_idx = 0
        self.slab_consumed = {}
        self.slab_store = {}
        self.cur_ci = 0
        self.scratch_wait_done = False
        self.nx_dma = 0
        self.nout = 0
        self.dbg_out = {}
        self.ta_setup = 0
        self.tv_xn_pending = 0
        self.pending_gu = None

    def op(self, eng, emit, waits=(), inc=None, inc16=None, nodrain=False):
        wl = []
        for (s, v) in waits:
            assert v <= self.cnt[s], f"wait on future {s}: {v} > {self.cnt[s]}"
            if v > 0:
                wl.append((s, v))
        incs = []
        if inc:
            self.cnt[inc] += 1
            incs.append((inc, 1))
        if inc16:
            self.cnt[inc16] += 16
            incs.append((inc16, 16))
        self.ops.append(_Op(eng, emit, wl, incs, nodrain=nodrain))
        return self.cnt[incs[0][0]] if incs else None

    def gop(self, emit, waits=()):
        return self.op("gpsimd", emit, waits=waits, inc="tg", nodrain=True)

    def c(self, s):
        return self.cnt[s]

    def cv(self, name, i=None, n=1):
        c0, w = _CONST_COLS[name]
        if i is None:
            return self.consts[:, c0:c0 + w]
        return self.consts[:, c0 + i:c0 + i + n]

    def finalize(self):
        nc = self.nc
        eng_ops = {"sync": [], "scalar": [], "vector": [], "tensor": [], "gpsimd": []}
        for o in self.ops:
            eng_ops[o.eng].append(o)

        def body(eng, name):
            do_drain = name in ("vector", "scalar")
            guaranteed = {}
            for o in eng_ops[name]:
                for (s, v) in o.waits:
                    if guaranteed.get(s, 0) >= v:
                        continue
                    eng.wait_ge(self.sems[s], v)
                    guaranteed[s] = v
                inst = o.emit(eng)
                for (s, v) in o.incs:
                    inst.then_inc(self.sems[s], v)
                if do_drain and not o.nodrain:
                    eng.drain()

        with nc.Block() as blk:
            blk.sync(lambda e: body(e, "sync"))
            blk.scalar(lambda e: body(e, "scalar"))
            blk.vector(lambda e: body(e, "vector"))
            blk.tensor(lambda e: body(e, "tensor"))
            blk.gpsimd(lambda e: body(e, "gpsimd"))
        return nc

    def setup(self):
        s = self
        s.op("sync", lambda e: e.dma_start(out=s.consts, in_=s.constsd), inc16="qc")
        s.op("sync", lambda e: e.dma_start(out=s.onesb, in_=s.onesbd), inc16="qc")
        s.op("sync", lambda e: e.dma_start(out=s.onesr, in_=s.onesrd), inc16="qc")
        qc = s.c("qc")
        s.x_load(0)
        s.op("tensor", lambda e: e.nop(), waits=[("qc", qc)])
        s.op("scalar", lambda e: e.activation(s.ed, s.cv("td"), AF.Exp),
             waits=[("qc", qc)], inc="ta")
        s.op("scalar", lambda e: e.activation(s.wdec, s.ed, AF.Exp, scale=-1.0),
             inc="ta")
        s.op("scalar", lambda e: e.activation(s.efa, s.cv("tf"), AF.Exp), inc="ta")
        s.ta_setup = s.c("ta")
        for t in s.carry + s.h1halo + s.h2halo:
            s.op("vector", (lambda t_: lambda e: e.memset(t_, 0.0))(t))
        s.op("vector", lambda e: e.memset(s.epsc, EPS))
        s.op("vector", lambda e: e.engine_nop(), inc="tv")

    def x_load(self, ci, waits=()):
        s = self
        c, sq = divmod(ci, s.nseq)
        s.op("gpsimd",
             (lambda sr: lambda e: e.dma_start(out=s.xh, in_=sr))(s.x5[sq, c]),
             waits=list(waits), inc16="qx", nodrain=True)
        s.nx_dma += 1

    def slab_dma(self, wname, m, used_elems=SLOT_F):
        s = self
        j = s.slab_idx
        s.slab_idx += 1
        slot = j % RING
        waits = []
        if j - RING in s.slab_consumed:
            waits.append(("tp", s.slab_consumed[j - RING]))
        if j - RING in s.slab_store:
            waits.append(("qs", s.slab_store[j - RING]))
        dst = s.ring[slot][:, 0:used_elems]
        sem = f"qw{slot}"
        fam32 = wname in s.scratch
        if fam32 and s.cur_ci > 0:
            if not s.scratch_wait_done:
                waits.append(("qs", s.c("qs")))
                s.scratch_wait_done = True
            src = s.scratch[wname][m]
            tgt = s.op("sync",
                       (lambda d2, so: lambda e: e.dma_start(out=d2, in_=so))(
                           dst, src),
                       waits=waits, inc16=sem)
        else:
            src = s.wdram[wname][m]
            tgt = s.op("gpsimd",
                       (lambda d2, so: lambda e: e.dma_start(out=d2, in_=so))(
                           dst, src),
                       waits=waits, inc16=sem, nodrain=True)
            if fam32:
                stgt = s.op("sync",
                            (lambda d2, so: lambda e: e.dma_start(
                                out=so, in_=d2))(dst, s.scratch[wname][m]),
                            waits=[(sem, tgt)], inc16="qs")
                s.slab_store[j] = stgt
        return {"j": j, "slot": slot, "wait": (sem, tgt)}

    def mm_group(self, bank_key, bank_ap, pairs, slabs, extra_waits=(),
                 open_start=True, close_stop=True, mark_slabs=None):
        s = self
        waits = list(extra_waits)
        for tok in slabs:
            waits.append(tok["wait"])
        if open_start:
            bf = s.bank_free.get(bank_key)
            if bf:
                waits.append(bf)
        n = len(pairs)
        gid = None
        for i, (lh, rh) in enumerate(pairs):
            st = open_start and (i == 0)
            sp = close_stop and (i == n - 1)
            gid = s.op("tensor",
                       (lambda l, r, st_, sp_, b: lambda e: e.matmul(
                           b, l, r, start=st_, stop=sp_))(lh, rh, st, sp, bank_ap),
                       waits=waits if i == 0 else (),
                       inc="tp" if sp else None)
        if close_stop:
            for tok in (slabs if mark_slabs is None else mark_slabs):
                s.slab_consumed[tok["j"]] = s.c("tp")
        return gid

    def evac(self, eng_sem, bank_key, gid, emit, extra_waits=()):
        s = self
        eng = {"tv": "vector", "ta": "scalar"}[eng_sem]
        waits = [("tp", gid)] + list(extra_waits)
        cnt = s.op(eng, emit, waits=waits, inc=eng_sem)
        s.bank_free[bank_key] = (eng_sem, cnt)
        return cnt

    def ln_stats(self, src_bf, sq_dst, ready, extra_stat=(), stage2=False):
        s = self
        x3 = src_bf.rearrange("p (a b) -> p a b", b=T)
        sq3 = sq_dst.rearrange("p (a b) -> p a b", b=T)
        tvsq = s.op("vector", lambda e: e.tensor_mul(sq3, x3, x3),
                    waits=list(ready), inc="tv")
        w0 = list(ready)
        if "S0" in s.bank_free:
            w0.append(s.bank_free["S0"])
        for dc in range(NDC):
            s.op("tensor",
                 (lambda d: lambda e: e.matmul(
                     s.pS0, s.onesb, src_bf[:, d * T:(d + 1) * T],
                     start=(d == 0), stop=(d == NDC - 1)))(dc),
                 waits=w0 if dc == 0 else ())
        w1 = [("tv", tvsq)]
        if "S1" in s.bank_free:
            w1.append(s.bank_free["S1"])
        gid = None
        for dc in range(NDC):
            gid = s.op("tensor",
                       (lambda d: lambda e: e.matmul(
                           s.pS1, s.onesb, sq_dst[:, d * T:(d + 1) * T],
                           start=(d == 0), stop=(d == NDC - 1)))(dc),
                       waits=w1 if dc == 0 else (),
                       inc="tp" if dc == NDC - 1 else None)
        vm = s.rB[0:1, :]
        vt = s.mrB[0:1, :]
        vmr = s.vecs
        s.op("vector", lambda e: e.tensor_scalar_mul(vm, s.pS0, 1.0 / D),
             waits=[("tp", gid)] + list(extra_stat))
        s.op("vector", lambda e: e.tensor_mul(vmr, vm, vm))
        tvv = s.op("vector",
                   lambda e: e.scalar_tensor_tensor(
                       vt, s.pS1, 1.0 / D, vmr, ALU.mult, ALU.subtract),
                   inc="tv")
        s.bank_free["S0"] = ("tv", tvv)
        s.bank_free["S1"] = ("tv", tvv)
        if stage2:
            tvv = s.op("vector", lambda e: e.tensor_scalar(
                vmr, vt, 1.0 + EPS, EPS * EPS, ALU.mult, ALU.add), inc="tv")
        tav = s.op("scalar", lambda e: e.activation(vt, vt, AF.Sqrt, bias=s.epsc),
                   waits=[("tv", tvv)], inc="ta")
        if stage2:
            tav = s.op("scalar", lambda e: e.activation(vmr, vmr, AF.Sqrt,
                                                        bias=0.0), inc="ta")
        s.op("vector", lambda e: e.reciprocal(vt, vt), waits=[("ta", tav)])
        if stage2:
            s.op("vector", lambda e: e.tensor_mul(vmr, vmr, vt))
            s.op("vector", lambda e: e.reciprocal(vmr, vmr))
        tvr = s.op("vector", lambda e: e.tensor_mul(vm, vm, vt), inc="tv")

        def finish(extra_copy=()):
            bc_waits = [("tv", tvr)]
            if ("D", 0) in s.bank_free:
                bc_waits.append(s.bank_free[("D", 0)])
            gid0 = s.op("tensor",
                        lambda e: e.matmul(s.pD[0], s.onesr, vt, start=True,
                                           stop=True),
                        waits=bc_waits, inc="tp")
            bc1_waits = []
            if ("D", 1) in s.bank_free:
                bc1_waits.append(s.bank_free[("D", 1)])
            gid1 = s.op("tensor",
                        lambda e: e.matmul(s.pD[1], s.onesr, vm, start=True,
                                           stop=True),
                        waits=bc1_waits, inc="tp")
            tv1 = s.op("vector", lambda e: e.tensor_copy(s.rB, s.pD[0]),
                       waits=[("tp", gid1)], inc="tv")
            ta1 = s.op("scalar", lambda e: e.activation(s.mrB, s.pD[1], AF.Copy),
                       waits=[("tp", gid1)] + list(extra_copy), inc="ta")
            s.bank_free[("D", 0)] = ("tv", tv1)
            s.bank_free[("D", 1)] = ("ta", ta1)
            return tv1, ta1

        def finish_r2():
            bc_waits = [s.bank_free[("D", 0)]]
            gid2 = s.op("tensor",
                        lambda e: e.matmul(s.pD[0], s.onesr, vmr, start=True,
                                           stop=True),
                        waits=bc_waits, inc="tp")
            tvr2 = s.op("vector", lambda e: e.tensor_copy(s.rB, s.pD[0]),
                        waits=[("tp", gid2)], inc="tv")
            s.bank_free[("D", 0)] = ("tv", tvr2)
            return tvr2
        return finish, (finish_r2 if stage2 else None)

    def emit_preln(self, ci, ready_tp=None, extra_stat=()):
        s = self
        ready = [("qx", 16 * (ci + 1))]
        if ci == 0:
            ready.append(("qc", s.c("qc")))
        if ready_tp is not None:
            ready.append(ready_tp)
        x_bf = s.xh[:, 0:8192]
        fin, fin_r2 = s.ln_stats(x_bf, s.wkvmix[:, 0:8192], ready,
                                 extra_stat=extra_stat, stage2=True)
        tv1, ta1 = fin(extra_copy=list(extra_stat))
        x3 = x_bf.rearrange("p (a b) -> p a b", b=T)
        xn3 = s.xn.rearrange("p (a b) -> p a b", b=T)

        def bc3(t):
            return t.unsqueeze(1).broadcast_to((128, NDC, T))
        s.op("vector", lambda e: e.tensor_mul(xn3, x3, bc3(s.rB)))
        s.tv_xn_pending = s.op("vector",
                               lambda e: e.tensor_sub(xn3, xn3, bc3(s.mrB)),
                               waits=[("ta", ta1)], inc="tv")
        fin_r2()

    def dve_ln_mix(self, tm_items, nw, nb, halo, tv1, ta1, tp_guard,
                   no_mean=False):
        s = self
        q3 = s.region[:, 0:8192].rearrange("p (a b) -> p a b", b=T)
        xn3 = s.xn.rearrange("p (a b) -> p a b", b=T)
        xh3 = s.xh.rearrange("p (a b) -> p a b", b=T)

        def bc3(t):
            return t.unsqueeze(1).broadcast_to((128, NDC, T))

        def bci(t16, n=T):
            return t16.unsqueeze(2).broadcast_to((128, NDC, n))
        w0 = [] if ta1 is None else [("ta", ta1)]
        if tp_guard:
            w0.append(("tp", tp_guard))
        if no_mean:
            s.tv_q_done = s.op("vector",
                               lambda e: e.tensor_mul(q3, xn3, bc3(s.rB)),
                               waits=w0, inc="tv")
            yield
        else:
            s.op("vector", lambda e: e.tensor_mul(q3, xn3, bc3(s.rB)),
                 waits=w0)
            yield
            s.tv_q_done = s.op("vector",
                               lambda e: e.tensor_sub(q3, q3, bc3(s.mrB)),
                               inc="tv")
            yield
        s.op("vector", lambda e: e.tensor_mul(xh3, q3, bci(s.cv(nw))))
        yield
        s.tv_h_done = s.op("vector",
                           lambda e: e.tensor_add(xh3, xh3, bci(s.cv(nb))),
                           inc="tv")
        yield
        s.op("vector", lambda e: e.tensor_sub(s.d0t, xh3[:, :, 0], halo))
        yield
        first3 = tm_items[0][0]
        s.op("vector", lambda e: e.tensor_sub(
            first3[:, :, 1:512], xh3[:, :, 1:512], xh3[:, :, 0:511]))
        yield
        for (dst3_, tmkey) in tm_items:
            c0 = dst3_[:, :, 0]
            s.op("vector", (lambda c0_, tk: lambda e: e.tensor_mul(
                c0_, s.d0t, s.cv(tk)))(c0, tmkey))
            yield
            s.op("vector", (lambda c0_: lambda e: e.tensor_add(
                c0_, c0_, halo))(c0))
            yield
        s.op("vector", lambda e: e.tensor_copy(halo, xh3[:, :, 511]))
        yield
        for (dst3_, tmkey) in tm_items[1:]:
            s.op("vector", (lambda ds, tk: lambda e: e.tensor_mul(
                ds[:, :, 1:512], first3[:, :, 1:512],
                bci(s.cv(tk), 511)))(dst3_, tmkey))
            yield
        tk0 = tm_items[0][1]
        s.op("vector", (lambda tk: lambda e: e.tensor_mul(
            first3[:, :, 1:512], first3[:, :, 1:512],
            bci(s.cv(tk), 511)))(tk0))
        yield
        for (dst3_, tmkey) in tm_items:
            s.op("vector", (lambda ds: lambda e: e.tensor_add(
                ds[:, :, 1:512], ds[:, :, 1:512], xh3[:, :, 0:511]))(dst3_),
                inc="tv")
            yield

    @staticmethod
    def drip(gen, n):
        for _ in range(n):
            try:
                next(gen)
            except StopIteration:
                return False
        return True

    def ffn_gu(self, f_range, tv_xn, state):
        s = self
        for f in f_range:
            par = f % 2
            if f % 2 == 0:
                if f == 0 and s.pending_gu is not None:
                    state["jg"], state["ju"] = s.pending_gu
                    s.pending_gu = None
                else:
                    state["jg"] = s.slab_dma("wg", f // 2)
                    state["ju"] = s.slab_dma("wu", f // 2)
            jg, ju = state["jg"], state["ju"]
            ringg, ringu = s.ring[jg["slot"]], s.ring[ju["slot"]]
            fd = f % 2
            pairsg = [(ringg[:, kc * 256 + fd * 128: kc * 256 + fd * 128 + 128],
                       s.xn[:, kc * T:(kc + 1) * T]) for kc in range(16)]
            gidg = s.mm_group(("A", par), s.pA[par], pairsg, [jg],
                              extra_waits=[("tv", tv_xn)])
            ta_g = s.evac("ta", ("A", par), gidg,
                          (lambda par_, f_: lambda e: e.activation(
                              s.gatet[par_], s.pA[par_], AF.Silu,
                              scale=s.cv("sgate", f_)))(par, f),
                          extra_waits=([("tv", state["gate_free"][par])]
                                       if par in state["gate_free"] else ()))
            pairsu = [(ringu[:, kc * 256 + fd * 128: kc * 256 + fd * 128 + 128],
                       s.xn[:, kc * T:(kc + 1) * T]) for kc in range(16)]
            gidu = s.mm_group(("B", par), s.pB[par], pairsu, [ju])
            fs = s.fuse[:, (f % 32) * T:((f % 32) + 1) * T]
            tvf = s.evac("tv", ("B", par), gidu,
                         (lambda par_, f_, fs_: lambda e: e.scalar_tensor_tensor(
                             fs_, s.pB[par_], s.cv("sup", f_),
                             s.gatet[par_], ALU.mult, ALU.mult))(par, f, fs),
                         extra_waits=[("ta", ta_g)])
            state["gate_free"][par] = tvf

    def ffn_down(self, half, tv_fuse, first_extra=()):
        s = self
        wname = "wd_lo" if half == 0 else "wd_hi"
        for dout in range(16):
            par = dout % 2
            jl = s.slab_dma(wname, dout)
            rl = s.ring[jl["slot"]]
            pairs = [(rl[:, kc * 128:(kc + 1) * 128],
                      s.fuse[:, kc * T:(kc + 1) * T]) for kc in range(32)]
            gid = s.mm_group(("A", par), s.pA[par], pairs, [jl],
                             extra_waits=[("tv", tv_fuse)])
            ew = list(first_extra) if dout == 0 else []
            if half == 0:
                x_bf = s.xh[:, 0:8192]
                emit = (lambda par_, d_: lambda e: e.scalar_tensor_tensor(
                    s.acc[:, d_ * T:(d_ + 1) * T], s.pA[par_],
                    s.cv("sdown", d_), x_bf[:, d_ * T:(d_ + 1) * T],
                    ALU.mult, ALU.add))(par, dout)
            else:
                emit = (lambda par_, d_: lambda e: e.scalar_tensor_tensor(
                    s.acc[:, d_ * T:(d_ + 1) * T], s.pA[par_],
                    s.cv("sdown", d_), s.acc[:, d_ * T:(d_ + 1) * T],
                    ALU.mult, ALU.add))(par, dout)
            s.evac("tv", ("A", par), gid, emit, extra_waits=ew)
            yield

    def chunk(self, ci):
        s = self
        s.cur_ci = ci
        c, sq = divmod(ci, s.nseq)
        xn3 = s.xn.rearrange("p (a b) -> p a b", b=T)
        acc3 = s.acc.rearrange("p (a b) -> p a b", b=T)
        xh3 = s.xh.rearrange("p (a b) -> p a b", b=T)
        reg = s.region
        last = s.nchunk * s.nseq - 1

        if ci == 0:
            s.emit_preln(0)
        tv_xn = s.tv_xn_pending
        tp_chunk_start = s.c("tp")

        gu_state = {"gate_free": {}, "jg": None, "ju": None}
        s.ffn_gu(range(0, 32), tv_xn, gu_state)
        tv_fuseA = s.c("tv")

        ew0 = [("qo", 16 * s.nout)] if s.nout > 0 else []
        downA = s.ffn_down(0, tv_fuseA, first_extra=ew0)
        for _ in range(16):
            next(downA)
        tp_downA = s.c("tp")
        tv_downA = s.c("tv")

        mixr3 = reg[:, 0:8192].rearrange("p (a b) -> p a b", b=T)
        mixk3 = reg[:, 8192:16384].rearrange("p (a b) -> p a b", b=T)
        mixv3 = reg[:, 16384:24576].rearrange("p (a b) -> p a b", b=T)
        g1 = None
        for i, f in enumerate(range(32, 64)):
            s.ffn_gu([f], tv_xn, gu_state)
            if i == 1:
                g1 = s.dve_ln_mix(
                    [(mixr3, "tmr"), (mixk3, "tmk"), (mixv3, "tmv")],
                    "n1w", "n1b", s.h1halo[sq], None, None, tp_chunk_start,
                    no_mean=True)
            if g1 is not None:
                s.drip(g1, 1)
        while s.drip(g1, 64):
            pass
        tv_mix = s.c("tv")
        tv_fuseB = max(gu_state["gate_free"].values())

        mixdst = {"r": reg[:, 0:8192], "k": reg[:, 8192:16384],
                  "v": reg[:, 16384:24576]}
        ring_rkv = {}
        rsig_free = {}
        kt_free = {}
        for dc in range(NDC):
            par = dc % 2
            if dc % 2 == 0:
                q = dc // 2
                for m in ("r", "k", "v"):
                    tok = s.slab_dma("w" + m, q)
                    ring_rkv[m] = (tok, s.ring[tok["slot"]])
            fd = dc % 2

            def lh(rt, kc):
                return rt[:, kc * 256 + fd * 128: kc * 256 + fd * 128 + 128]

            jr, rr = ring_rkv["r"]
            gidr = s.mm_group(("A", par), s.pA[par],
                              [(lh(rr, kc), mixdst["r"][:, kc * T:(kc + 1) * T])
                               for kc in range(16)], [jr],
                              extra_waits=[("tv", tv_mix)])
            s.evac("ta", ("A", par), gidr,
                   (lambda par_: lambda e: e.activation(
                       s.rsig[par_], s.pA[par_], AF.Sigmoid))(par),
                   extra_waits=([("tv", rsig_free[par])]
                                if par in rsig_free else ()))
            jk, rk = ring_rkv["k"]
            gidk = s.mm_group(("B", par), s.pB[par],
                              [(lh(rk, kc), mixdst["k"][:, kc * T:(kc + 1) * T])
                               for kc in range(16)], [jk])
            s.evac("ta", ("B", par), gidk,
                   (lambda par_: lambda e: e.copy(s.kt[par_], s.pB[par_]))(par),
                   extra_waits=([("tv", kt_free[par])]
                                if par in kt_free else ()))
            jv, rv = ring_rkv["v"]
            gidv = s.mm_group(("D", par), s.pD[par],
                              [(lh(rv, kc), mixdst["v"][:, kc * T:(kc + 1) * T])
                               for kc in range(16)], [jv])
            kt_free[par] = s.evac(
                "tv", ("D", par), gidv,
                (lambda par_: lambda e: e.tensor_mul(
                    s.kvt, s.pD[par_], s.kt[par_]))(par),
                extra_waits=[("ta", s.c("ta"))])
            scan_waits = [("ta", s.ta_setup)] if ci == 0 and dc == 0 else ()
            s.op("vector",
                 (lambda d: lambda e: e.tensor_tensor_scan(
                     s.kvt, s.wdec[:, d:d + 1].broadcast_to((128, T)), s.kvt,
                     s.carry[sq][:, d:d + 1], ALU.mult, ALU.add))(dc),
                 waits=scan_waits)
            s.op("vector",
                 (lambda d: lambda e: e.tensor_copy(
                     s.carry[sq][:, d:d + 1], s.kvt[:, T - 1:T]))(dc))
            wkv_dst = s.wkvmix[:, dc * T:(dc + 1) * T]
            rsig_free[par] = s.op(
                "vector",
                (lambda par_, ds: lambda e: e.tensor_mul(
                    ds, s.rsig[par_], s.kvt))(par, wkv_dst), inc="tv")
            if c == 0:
                s.op("vector",
                     (lambda d: lambda e: e.tensor_scalar_mul(
                         s.ut[:, 0:1], s.kvt[:, 0:1], s.efa[:, d:d + 1]))(dc))
                s.op("vector",
                     (lambda par_: lambda e: e.tensor_mul(
                         s.ut[:, 0:1], s.ut[:, 0:1],
                         s.rsig[par_][:, 0:1]))(par))
                rsig_free[par] = s.op(
                    "vector",
                    (lambda ds: lambda e: e.tensor_add(
                        ds[:, 0:1], ds[:, 0:1], s.ut[:, 0:1]))(wkv_dst),
                    inc="tv")
        tv_wkv = s.op("vector", lambda e: e.engine_nop(), inc="tv")

        jo = None
        ringo = None
        for dc in range(NDC):
            par = dc % 2
            if dc % 2 == 0:
                jo = s.slab_dma("wo", dc // 2)
                ringo = s.ring[jo["slot"]]
            fd = dc % 2
            pairs = [(ringo[:, kc * 256 + fd * 128: kc * 256 + fd * 128 + 128],
                      s.wkvmix[:, kc * T:(kc + 1) * T]) for kc in range(16)]
            gid = s.mm_group(("A", par), s.pA[par], pairs, [jo],
                             extra_waits=[("tv", tv_wkv)])
            s.evac("tv", ("A", par), gid,
                   (lambda par_, d: lambda e: e.tensor_add(
                       s.xn[:, d * T:(d + 1) * T], s.pA[par_],
                       s.xn[:, d * T:(d + 1) * T]))(par, dc))
        tv_y = s.c("tv")
        tp_wo = s.c("tp")

        mixed3 = s.wkvmix.rearrange("p (a b) -> p a b", b=T)
        downB = s.ffn_down(1, tv_fuseB)
        g2 = None
        fin2 = None
        tv2 = ta2 = None
        tv_q1_done = s.tv_q_done
        for i in range(16):
            next(downB)
            if i == 0:
                fin2, _ = s.ln_stats(s.xn, reg[:, 0:8192],
                                     [("tv", tv_y), ("tp", tp_wo)])
            if i == 1:
                tv2, ta2 = fin2(extra_copy=[("tv", tv_q1_done)])
                g2 = s.dve_ln_mix(
                    [(mixed3, "cmm")],
                    "n2w", "n2b", s.h2halo[sq], tv2, ta2, tp_wo)
            if g2 is not None:
                s.drip(g2, 2)
        while s.drip(g2, 64):
            pass
        tv_mixed = s.c("tv")
        tv_h2h = s.tv_h_done
        if ci < last:
            s.x_load(ci + 1, waits=[("tv", tv_mixed)])
        tp_downB = s.c("tp")

        rcm = s.fuse[:, 0:8192]
        jc = None
        ringc = None
        for dc in range(NDC):
            par = dc % 2
            if dc % 2 == 0:
                jc = s.slab_dma("wcr", dc // 2)
                ringc = s.ring[jc["slot"]]
            fd = dc % 2
            pairs = [(ringc[:, kc * 256 + fd * 128: kc * 256 + fd * 128 + 128],
                      s.wkvmix[:, kc * T:(kc + 1) * T]) for kc in range(16)]
            gid = s.mm_group(("A", par), s.pA[par], pairs, [jc],
                             extra_waits=[("tv", tv_mixed)])
            s.evac("ta", ("A", par), gid,
                   (lambda par_, d: lambda e: e.activation(
                       rcm[:, d * T:(d + 1) * T], s.pA[par_],
                       AF.Sigmoid))(par, dc))

        tv_accy = s.op("vector", lambda e: e.tensor_add(acc3, acc3, xn3),
                       inc="tv")

        jk2 = None
        ringk2 = None
        for f in range(48):
            par = f % 2
            if f % 2 == 0:
                jk2 = s.slab_dma("wck", f // 2)
                ringk2 = s.ring[jk2["slot"]]
            fd = f % 2
            pairs = [(ringk2[:, kc * 256 + fd * 128: kc * 256 + fd * 128 + 128],
                      s.wkvmix[:, kc * T:(kc + 1) * T]) for kc in range(16)]
            gid = s.mm_group(("A", par), s.pA[par], pairs, [jk2],
                             extra_waits=[("tv", tv_mixed)])
            s.evac("ta", ("A", par), gid,
                   (lambda par_, f_: lambda e: e.activation(
                       reg[:, f_ * T:(f_ + 1) * T], s.pA[par_],
                       AF.Square))(par, f),
                   extra_waits=[("tv", max(tv_h2h, tv_accy))]
                   if f < 2 else ())
        ta_kcm = s.c("ta")
        tp_wck = s.c("tp")

        fin_next = None
        for dout in range(NDC):
            par = dout % 2
            jvl = s.slab_dma("wcv_lo", dout, used_elems=3072)
            rvl = s.ring[jvl["slot"]]
            pairs1 = [(rvl[:, kc * 128:(kc + 1) * 128],
                       reg[:, kc * T:(kc + 1) * T]) for kc in range(24)]
            s.mm_group(("A", par), s.pA[par], pairs1, [jvl],
                       extra_waits=[("ta", ta_kcm)],
                       close_stop=False, mark_slabs=[])
            jvh = s.slab_dma("wcv_hi", dout, used_elems=3072)
            rvh = s.ring[jvh["slot"]]
            pairs2 = [(rvh[:, kc * 128:(kc + 1) * 128],
                       reg[:, (24 + kc) * T:(25 + kc) * T]) for kc in range(24)]
            gid = s.mm_group(("A", par), s.pA[par], pairs2, [jvh],
                             open_start=False, mark_slabs=[jvl, jvh])
            s.op("vector",
                 (lambda par_, d: lambda e: e.tensor_mul(
                     s.t3, s.pA[par_], rcm[:, d * T:(d + 1) * T]))(par, dout),
                 waits=[("tp", gid)])
            tvv = s.op("vector",
                       (lambda d: lambda e: e.tensor_add(
                           s.acc[:, d * T:(d + 1) * T],
                           s.acc[:, d * T:(d + 1) * T], s.t3))(dout),
                       inc="tv")
            s.bank_free[("A", par)] = ("tv", tvv)
            if ci < last:
                if dout == 0:
                    fin_next, fin_r2n = s.ln_stats(
                        s.xh[:, 0:8192], s.wkvmix[:, 0:8192],
                        [("qx", 16 * (ci + 2)), ("tp", tp_wck)],
                        extra_stat=(), stage2=True)
                elif dout == 4:
                    tv1n, ta1n = fin_next(extra_copy=[("tv", s.tv_q_done)])
                    x3n = s.xh[:, 0:8192].rearrange("p (a b) -> p a b", b=T)

                    def bc3(t):
                        return t.unsqueeze(1).broadcast_to((128, NDC, T))
                    s.op("vector",
                         lambda e: e.tensor_mul(xn3, x3n, bc3(s.rB)),
                         waits=())
                    s.tv_xn_pending = s.op(
                        "vector",
                        lambda e: e.tensor_sub(xn3, xn3, bc3(s.mrB)),
                        waits=[("ta", ta1n)], inc="tv")
                    fin_r2n()
        tv_done = s.c("tv")

        if ci < last:
            s.pending_gu = (s.slab_dma("wg", 0), s.slab_dma("wu", 0))

        s.op("sync",
             (lambda c_, sq_: lambda e: e.dma_start(
                 out=s.out5[sq_, c_], in_=s.acc))(c, sq),
             waits=[("tv", tv_done)], inc16="qo")
        s.nout += 1

    def build(self):
        self.setup()
        for ci in range(self.nchunk * self.nseq):
            self.chunk(ci)
        qo = self.c("qo")
        self.op("sync", lambda e: e.nop(), waits=[("qo", qo)])
        return self.finalize()


def host_inputs(inputs, nchunk=4, nseq=NSEQ, cores=NCORE):
    x = np.asarray(inputs["L0_attn_x"])
    seqlen = nchunk * T

    wcv = np.asarray(inputs["L0_attn_Wcv"])
    wd = np.asarray(inputs["L0_ffn_w_down"])
    wb = {
        "wr": _block_w(np.asarray(inputs["L0_attn_Wr"]), 256),
        "wk": _block_w(np.asarray(inputs["L0_attn_Wk"]), 256),
        "wv": _block_w(np.asarray(inputs["L0_attn_Wv"]), 256),
        "wo": _block_w(np.asarray(inputs["L0_attn_Wo"]), 256),
        "wcr": _block_w(np.asarray(inputs["L0_attn_Wcr"]), 256),
        "wck": _block_w(np.asarray(inputs["L0_attn_Wck"]), 256),
        "wcv_lo": _block_w(wcv[:3072], 128),
        "wcv_hi": _block_w(wcv[3072:], 128),
        "wg": _block_w(np.asarray(inputs["L0_ffn_w_gate"]), 256),
        "wu": _block_w(np.asarray(inputs["L0_ffn_w_up"]), 256),
        "wd_lo": _block_w(wd[:4096], 128),
        "wd_hi": _block_w(wd[4096:], 128),
    }
    consts = np.concatenate([
        _vec128(inputs["L0_attn_norm1_w"]), _vec128(inputs["L0_attn_norm1_b"]),
        _vec128(inputs["L0_attn_norm2_w"]), _vec128(inputs["L0_attn_norm2_b"]),
        _vec128(inputs["L0_attn_time_decay"]),
        _vec128(inputs["L0_attn_time_faaaa"]),
        _vec128(inputs["L0_attn_time_mix_r"]),
        _vec128(inputs["L0_attn_time_mix_k"]),
        _vec128(inputs["L0_attn_time_mix_v"]),
        _vec128(inputs["L0_attn_cm_mix"]),
        _vec128(inputs["L0_ffn_s_down"]),
        _vec128(inputs["L0_ffn_s_gate"]), _vec128(inputs["L0_ffn_s_up"]),
    ], axis=1).astype(np.float32)
    consts = np.ascontiguousarray(consts)
    onesb = np.ones((128, 1), dtype=bfdt)
    onesr = np.ones((1, 128), dtype=np.float32)

    in_maps = []
    for core in range(cores):
        xs = x[core * nseq:(core + 1) * nseq, :seqlen]
        x5 = np.ascontiguousarray(
            xs.reshape(nseq, nchunk, T, NDC, 128).transpose(0, 1, 4, 3, 2)
            .reshape(nseq, nchunk, 128, NDC * T))
        m = {"x5": x5, "consts": consts, "onesb": onesb, "onesr": onesr}
        m.update(wb)
        in_maps.append(m)
    return in_maps


def host_gather(results, nchunk=4, nseq=NSEQ):
    outs = []
    for r in results:
        o = np.asarray(r["out5"])
        o = o.reshape(nseq, nchunk, 128, NDC, T).transpose(0, 1, 4, 3, 2)
        outs.append(o.reshape(nseq, nchunk * T, D))
    return np.concatenate(outs, axis=0)


def kernel(**inputs):
    b = Builder(nchunk=4, nseq=NSEQ)
    nc = b.build()
    in_maps = host_inputs(inputs)
    res = run_bass_kernel_spmd(nc, in_maps, core_ids=list(range(NCORE)))
    return host_gather(res.results).astype(np.float32)


# version 14
# speedup vs baseline: 1.2562x; 1.1022x over previous
import numpy as np
import ml_dtypes
import concourse.bass as bass
import concourse.mybir as mybir
from concourse.bass_utils import run_bass_kernel_spmd

F32 = mybir.dt.float32
BF16 = mybir.dt.bfloat16
I8 = mybir.dt.int8
AF = mybir.ActivationFunctionType
ALU = mybir.AluOpType

D = 2048
NDC = D // 128
T = 512
NCORE = 8
NSEQ = 2
EPS = 1e-5
RING = 3
SLOT_F = 4096

bfdt = ml_dtypes.bfloat16

_CONST_COLS = {}
_c = 0
for _n, _w in [("n1w", 16), ("n1b", 16), ("n2w", 16), ("n2b", 16),
               ("td", 16), ("tf", 16), ("tmr", 16), ("tmk", 16),
               ("tmv", 16), ("cmm", 16), ("sdown", 16),
               ("sgate", 64), ("sup", 64)]:
    _CONST_COLS[_n] = (_c, _w)
    _c += _w
NCONST = _c


def _vec128(v):
    v = np.asarray(v).reshape(-1)
    return np.ascontiguousarray(v.reshape(-1, 128).T)


def _block_w(w, wm):
    K, M = w.shape
    nk, nm = K // 128, M // wm
    return np.ascontiguousarray(
        w.reshape(nk, 128, nm, wm).transpose(2, 1, 0, 3).reshape(nm, 128, nk * wm))


class _Op:
    __slots__ = ("eng", "emit", "waits", "incs", "nodrain")

    def __init__(self, eng, emit, waits, incs, nodrain=False):
        self.eng, self.emit, self.waits, self.incs = eng, emit, waits, incs
        self.nodrain = nodrain


class Builder:
    def __init__(self, nchunk=4, nseq=NSEQ, debug=False):
        self.nchunk = nchunk
        self.nseq = nseq
        self.debug = debug
        nc = bass.Bass("TRN2", target_bir_lowering=False)
        self.nc = nc
        self.ops = []
        self.cnt = {}
        self.sems = {}
        semnames = ["qx", "qc", "qo", "tp", "tv", "ta", "tg"] + [
            f"qw{i}" for i in range(RING)] + [f"qs{i}" for i in range(RING)]
        for s in semnames:
            self.sems[s] = nc.alloc_semaphore(s)
            self.cnt[s] = 0

        dt = nc.dram_tensor
        self.x5 = dt("x5", [nseq, nchunk, 128, NDC * T], F32, kind="ExternalInput").ap()
        self.out5 = dt("out5", [nseq, nchunk, 128, NDC * T], F32,
                       kind="ExternalOutput").ap()
        self.wdram = {}
        for name, shape, dtype in [
            ("wr", [8, 128, 4096], F32), ("wk", [8, 128, 4096], F32),
            ("wv", [8, 128, 4096], F32), ("wo", [8, 128, 4096], F32),
            ("wcr", [8, 128, 4096], F32), ("wck", [24, 128, 4096], F32),
            ("wcv_lo", [16, 128, 3072], F32), ("wcv_hi", [16, 128, 3072], F32),
            ("wg", [32, 128, 4096], I8), ("wu", [32, 128, 4096], I8),
            ("wd_lo", [16, 128, 4096], I8), ("wd_hi", [16, 128, 4096], I8),
        ]:
            self.wdram[name] = dt(name, shape, dtype, kind="ExternalInput").ap()
        self.scratch = {}
        for name in ("wr", "wk", "wv", "wo", "wcr", "wck",
                     "wcv_lo", "wcv_hi"):
            shape = list(self.wdram[name].shape)
            self.scratch[name] = dt("sc_" + name, shape, BF16,
                                    kind="Internal").ap()
        self.constsd = dt("consts", [128, NCONST], F32, kind="ExternalInput").ap()
        self.onesbd = dt("onesb", [128, 1], BF16, kind="ExternalInput").ap()
        self.onesrd = dt("onesr", [1, 128], F32, kind="ExternalInput").ap()

        al = nc.alloc_sbuf_tensor
        self.ring = [al(f"ring{i}", [128, SLOT_F], BF16).ap() for i in range(RING)]
        self.xh = al("xh", [128, 8192], BF16).ap()
        self.xn = al("xn", [128, 8192], BF16).ap()
        self.fuse = al("fuse", [128, 16384], BF16).ap()
        self.region = al("region", [128, 24576], BF16).ap()
        self.wkvmix = al("wkvmix", [128, 8192], BF16).ap()
        self.acc = al("acc", [128, 8192], F32).ap()
        self.consts = al("consts_t", [128, NCONST], F32).ap()
        self.onesb = al("onesb_t", [128, 1], BF16).ap()
        self.onesr = al("onesr_t", [1, 128], F32).ap()
        self.ed = al("ed_t", [128, 16], F32).ap()
        self.wdec = al("wdec_t", [128, 16], F32).ap()
        self.efa = al("efa_t", [128, 16], F32).ap()
        self.carry = [al(f"carry{s}", [128, 16], F32).ap() for s in range(nseq)]
        self.h1halo = [al(f"h1halo{s}", [128, 16], BF16).ap() for s in range(nseq)]
        self.h2halo = [al(f"h2halo{s}", [128, 16], BF16).ap() for s in range(nseq)]
        self.rB = al("rB", [128, T], F32).ap()
        self.mrB = al("mrB", [128, T], F32).ap()
        self.vecs = al("vecs", [1, T], F32).ap()
        self.rsig = [al(f"rsig{i}", [128, T], F32).ap() for i in range(2)]
        self.kt = [al(f"kt{i}", [128, T], F32).ap() for i in range(2)]
        self.kvt = al("kvt", [128, T], F32).ap()
        self.gatet = [al(f"gatet{i}", [128, T], BF16).ap() for i in range(2)]
        self.ut = al("ut_t", [128, 16], F32).ap()
        self.d0t = al("d0t", [128, 16], BF16).ap()
        self.t3 = al("t3", [128, T], F32).ap()
        self.epsc = al("epsc", [1, 1], F32).ap()

        ps = nc.alloc_psum_tensor
        self.pA = [ps(f"pA{i}", [128, T], F32).ap() for i in range(2)]
        self.pB = [ps(f"pB{i}", [128, T], F32).ap() for i in range(2)]
        self.pD = [ps(f"pD{i}", [128, T], F32).ap() for i in range(2)]
        self.pS0 = ps("pS0", [1, T], F32).ap()
        self.pS1 = ps("pS1", [1, T], F32).ap()

        self.bank_free = {}
        self.slab_idx = 0
        self.slab_consumed = {}
        self.slab_store = {}
        self.store_last = {}
        self.cur_ci = 0
        self.scratch_wait_done = False
        self.nx_dma = 0
        self.nout = 0
        self.dbg_out = {}
        self.ta_setup = 0
        self.tv_xn_pending = 0
        self.pending_gu = None

    def op(self, eng, emit, waits=(), inc=None, inc16=None, nodrain=False):
        wl = []
        for (s, v) in waits:
            assert v <= self.cnt[s], f"wait on future {s}: {v} > {self.cnt[s]}"
            if v > 0:
                wl.append((s, v))
        incs = []
        if inc:
            self.cnt[inc] += 1
            incs.append((inc, 1))
        if inc16:
            self.cnt[inc16] += 16
            incs.append((inc16, 16))
        self.ops.append(_Op(eng, emit, wl, incs, nodrain=nodrain))
        return self.cnt[incs[0][0]] if incs else None

    def gop(self, emit, waits=()):
        return self.op("gpsimd", emit, waits=waits, inc="tg", nodrain=True)

    def c(self, s):
        return self.cnt[s]

    def cv(self, name, i=None, n=1):
        c0, w = _CONST_COLS[name]
        if i is None:
            return self.consts[:, c0:c0 + w]
        return self.consts[:, c0 + i:c0 + i + n]

    def finalize(self):
        nc = self.nc
        eng_ops = {"sync": [], "scalar": [], "vector": [], "tensor": [], "gpsimd": []}
        for o in self.ops:
            eng_ops[o.eng].append(o)

        def body(eng, name):
            do_drain = name in ("vector", "scalar")
            guaranteed = {}
            for o in eng_ops[name]:
                for (s, v) in o.waits:
                    if guaranteed.get(s, 0) >= v:
                        continue
                    eng.wait_ge(self.sems[s], v)
                    guaranteed[s] = v
                inst = o.emit(eng)
                for (s, v) in o.incs:
                    inst.then_inc(self.sems[s], v)
                if do_drain and not o.nodrain:
                    eng.drain()

        with nc.Block() as blk:
            blk.sync(lambda e: body(e, "sync"))
            blk.scalar(lambda e: body(e, "scalar"))
            blk.vector(lambda e: body(e, "vector"))
            blk.tensor(lambda e: body(e, "tensor"))
            blk.gpsimd(lambda e: body(e, "gpsimd"))
        return nc

    def setup(self):
        s = self
        s.op("sync", lambda e: e.dma_start(out=s.consts, in_=s.constsd), inc16="qc")
        s.op("sync", lambda e: e.dma_start(out=s.onesb, in_=s.onesbd), inc16="qc")
        s.op("sync", lambda e: e.dma_start(out=s.onesr, in_=s.onesrd), inc16="qc")
        qc = s.c("qc")
        s.x_load(0)
        s.op("tensor", lambda e: e.nop(), waits=[("qc", qc)])
        s.op("scalar", lambda e: e.activation(s.ed, s.cv("td"), AF.Exp),
             waits=[("qc", qc)], inc="ta")
        s.op("scalar", lambda e: e.activation(s.wdec, s.ed, AF.Exp, scale=-1.0),
             inc="ta")
        s.op("scalar", lambda e: e.activation(s.efa, s.cv("tf"), AF.Exp), inc="ta")
        s.ta_setup = s.c("ta")
        for t in s.carry + s.h1halo + s.h2halo:
            s.op("vector", (lambda t_: lambda e: e.memset(t_, 0.0))(t))
        s.op("vector", lambda e: e.memset(s.epsc, EPS))
        s.op("vector", lambda e: e.engine_nop(), inc="tv")

    def x_load(self, ci, waits=()):
        s = self
        c, sq = divmod(ci, s.nseq)
        s.op("gpsimd",
             (lambda sr: lambda e: e.dma_start(out=s.xh, in_=sr))(s.x5[sq, c]),
             waits=list(waits), inc16="qx", nodrain=True)
        s.nx_dma += 1

    def slab_dma(self, wname, m, used_elems=SLOT_F):
        s = self
        j = s.slab_idx
        s.slab_idx += 1
        slot = j % RING
        waits = []
        if j - RING in s.slab_consumed:
            waits.append(("tp", s.slab_consumed[j - RING]))
        if j - RING in s.slab_store:
            waits.append(s.slab_store[j - RING])
        dst = s.ring[slot][:, 0:used_elems]
        sem = f"qw{slot}"
        fam32 = wname in s.scratch
        if fam32 and s.cur_ci > 0:
            if not s.scratch_wait_done:
                waits.extend(s.store_last.values())
                s.scratch_wait_done = True
            src = s.scratch[wname][m]
            tgt = s.op("sync",
                       (lambda d2, so: lambda e: e.dma_start(out=d2, in_=so))(
                           dst, src),
                       waits=waits, inc16=sem)
        else:
            src = s.wdram[wname][m]
            tgt = s.op("gpsimd",
                       (lambda d2, so: lambda e: e.dma_start(out=d2, in_=so))(
                           dst, src),
                       waits=waits, inc16=sem, nodrain=True)
            if fam32:
                ssem = f"qs{slot}"
                stgt = s.op("sync",
                            (lambda d2, so: lambda e: e.dma_start(
                                out=so, in_=d2))(dst, s.scratch[wname][m]),
                            waits=[(sem, tgt)], inc16=ssem)
                s.slab_store[j] = (ssem, stgt)
                s.store_last[slot] = (ssem, stgt)
        return {"j": j, "slot": slot, "wait": (sem, tgt)}

    def mm_group(self, bank_key, bank_ap, pairs, slabs, extra_waits=(),
                 open_start=True, close_stop=True, mark_slabs=None):
        s = self
        waits = list(extra_waits)
        for tok in slabs:
            waits.append(tok["wait"])
        if open_start:
            bf = s.bank_free.get(bank_key)
            if bf:
                waits.append(bf)
        n = len(pairs)
        gid = None
        for i, (lh, rh) in enumerate(pairs):
            st = open_start and (i == 0)
            sp = close_stop and (i == n - 1)
            gid = s.op("tensor",
                       (lambda l, r, st_, sp_, b: lambda e: e.matmul(
                           b, l, r, start=st_, stop=sp_))(lh, rh, st, sp, bank_ap),
                       waits=waits if i == 0 else (),
                       inc="tp" if sp else None)
        if close_stop:
            for tok in (slabs if mark_slabs is None else mark_slabs):
                s.slab_consumed[tok["j"]] = s.c("tp")
        return gid

    def evac(self, eng_sem, bank_key, gid, emit, extra_waits=()):
        s = self
        eng = {"tv": "vector", "ta": "scalar"}[eng_sem]
        waits = [("tp", gid)] + list(extra_waits)
        cnt = s.op(eng, emit, waits=waits, inc=eng_sem)
        s.bank_free[bank_key] = (eng_sem, cnt)
        return cnt

    def ln_stats(self, src_bf, sq_dst, ready, extra_stat=(), stage2=False):
        s = self
        x3 = src_bf.rearrange("p (a b) -> p a b", b=T)
        sq3 = sq_dst.rearrange("p (a b) -> p a b", b=T)
        tvsq = s.op("vector", lambda e: e.tensor_mul(sq3, x3, x3),
                    waits=list(ready), inc="tv")
        w0 = list(ready)
        if "S0" in s.bank_free:
            w0.append(s.bank_free["S0"])
        for dc in range(NDC):
            s.op("tensor",
                 (lambda d: lambda e: e.matmul(
                     s.pS0, s.onesb, src_bf[:, d * T:(d + 1) * T],
                     start=(d == 0), stop=(d == NDC - 1)))(dc),
                 waits=w0 if dc == 0 else ())
        w1 = [("tv", tvsq)]
        if "S1" in s.bank_free:
            w1.append(s.bank_free["S1"])
        gid = None
        for dc in range(NDC):
            gid = s.op("tensor",
                       (lambda d: lambda e: e.matmul(
                           s.pS1, s.onesb, sq_dst[:, d * T:(d + 1) * T],
                           start=(d == 0), stop=(d == NDC - 1)))(dc),
                       waits=w1 if dc == 0 else (),
                       inc="tp" if dc == NDC - 1 else None)
        vm = s.rB[0:1, :]
        vt = s.mrB[0:1, :]
        vmr = s.vecs
        s.op("vector", lambda e: e.tensor_scalar_mul(vm, s.pS0, 1.0 / D),
             waits=[("tp", gid)] + list(extra_stat))
        s.op("vector", lambda e: e.tensor_mul(vmr, vm, vm))
        tvv = s.op("vector",
                   lambda e: e.scalar_tensor_tensor(
                       vt, s.pS1, 1.0 / D, vmr, ALU.mult, ALU.subtract),
                   inc="tv")
        s.bank_free["S0"] = ("tv", tvv)
        s.bank_free["S1"] = ("tv", tvv)
        if stage2:
            tvv = s.op("vector", lambda e: e.tensor_scalar(
                vmr, vt, 1.0 + EPS, EPS * EPS, ALU.mult, ALU.add), inc="tv")
        tav = s.op("scalar", lambda e: e.activation(vt, vt, AF.Sqrt, bias=s.epsc),
                   waits=[("tv", tvv)], inc="ta")
        if stage2:
            tav = s.op("scalar", lambda e: e.activation(vmr, vmr, AF.Sqrt,
                                                        bias=0.0), inc="ta")
        s.op("vector", lambda e: e.reciprocal(vt, vt), waits=[("ta", tav)])
        if stage2:
            s.op("vector", lambda e: e.tensor_mul(vmr, vmr, vt))
            s.op("vector", lambda e: e.reciprocal(vmr, vmr))
        tvr = s.op("vector", lambda e: e.tensor_mul(vm, vm, vt), inc="tv")

        def finish(extra_copy=()):
            bc_waits = [("tv", tvr)]
            if ("D", 0) in s.bank_free:
                bc_waits.append(s.bank_free[("D", 0)])
            gid0 = s.op("tensor",
                        lambda e: e.matmul(s.pD[0], s.onesr, vt, start=True,
                                           stop=True),
                        waits=bc_waits, inc="tp")
            bc1_waits = []
            if ("D", 1) in s.bank_free:
                bc1_waits.append(s.bank_free[("D", 1)])
            gid1 = s.op("tensor",
                        lambda e: e.matmul(s.pD[1], s.onesr, vm, start=True,
                                           stop=True),
                        waits=bc1_waits, inc="tp")
            tv1 = s.op("vector", lambda e: e.tensor_copy(s.rB, s.pD[0]),
                       waits=[("tp", gid1)], inc="tv")
            ta1 = s.op("scalar", lambda e: e.activation(s.mrB, s.pD[1], AF.Copy),
                       waits=[("tp", gid1)] + list(extra_copy), inc="ta")
            s.bank_free[("D", 0)] = ("tv", tv1)
            s.bank_free[("D", 1)] = ("ta", ta1)
            return tv1, ta1

        def finish_r2():
            bc_waits = [s.bank_free[("D", 0)]]
            gid2 = s.op("tensor",
                        lambda e: e.matmul(s.pD[0], s.onesr, vmr, start=True,
                                           stop=True),
                        waits=bc_waits, inc="tp")
            tvr2 = s.op("vector", lambda e: e.tensor_copy(s.rB, s.pD[0]),
                        waits=[("tp", gid2)], inc="tv")
            s.bank_free[("D", 0)] = ("tv", tvr2)
            return tvr2
        return finish, (finish_r2 if stage2 else None)

    def emit_preln(self, ci, ready_tp=None, extra_stat=()):
        s = self
        ready = [("qx", 16 * (ci + 1))]
        if ci == 0:
            ready.append(("qc", s.c("qc")))
        if ready_tp is not None:
            ready.append(ready_tp)
        x_bf = s.xh[:, 0:8192]
        fin, fin_r2 = s.ln_stats(x_bf, s.wkvmix[:, 0:8192], ready,
                                 extra_stat=extra_stat, stage2=True)
        tv1, ta1 = fin(extra_copy=list(extra_stat))
        x3 = x_bf.rearrange("p (a b) -> p a b", b=T)
        xn3 = s.xn.rearrange("p (a b) -> p a b", b=T)

        def bc3(t):
            return t.unsqueeze(1).broadcast_to((128, NDC, T))
        s.op("vector", lambda e: e.tensor_mul(xn3, x3, bc3(s.rB)))
        s.tv_xn_pending = s.op("vector",
                               lambda e: e.tensor_sub(xn3, xn3, bc3(s.mrB)),
                               waits=[("ta", ta1)], inc="tv")
        fin_r2()

    def dve_ln_mix(self, tm_items, nw, nb, halo, tv1, ta1, tp_guard,
                   no_mean=False):
        s = self
        q3 = s.region[:, 0:8192].rearrange("p (a b) -> p a b", b=T)
        xn3 = s.xn.rearrange("p (a b) -> p a b", b=T)
        xh3 = s.xh.rearrange("p (a b) -> p a b", b=T)

        def bc3(t):
            return t.unsqueeze(1).broadcast_to((128, NDC, T))

        def bci(t16, n=T):
            return t16.unsqueeze(2).broadcast_to((128, NDC, n))
        w0 = [] if ta1 is None else [("ta", ta1)]
        if tp_guard:
            w0.append(("tp", tp_guard))
        if no_mean:
            s.tv_q_done = s.op("vector",
                               lambda e: e.tensor_mul(q3, xn3, bc3(s.rB)),
                               waits=w0, inc="tv")
            yield
        else:
            s.op("vector", lambda e: e.tensor_mul(q3, xn3, bc3(s.rB)),
                 waits=w0)
            yield
            s.tv_q_done = s.op("vector",
                               lambda e: e.tensor_sub(q3, q3, bc3(s.mrB)),
                               inc="tv")
            yield
        s.op("vector", lambda e: e.tensor_mul(xh3, q3, bci(s.cv(nw))))
        yield
        s.tv_h_done = s.op("vector",
                           lambda e: e.tensor_add(xh3, xh3, bci(s.cv(nb))),
                           inc="tv")
        yield
        s.op("vector", lambda e: e.tensor_sub(s.d0t, xh3[:, :, 0], halo))
        yield
        first3 = tm_items[0][0]
        s.op("vector", lambda e: e.tensor_sub(
            first3[:, :, 1:512], xh3[:, :, 1:512], xh3[:, :, 0:511]))
        yield
        for (dst3_, tmkey) in tm_items:
            c0 = dst3_[:, :, 0]
            s.op("vector", (lambda c0_, tk: lambda e: e.tensor_mul(
                c0_, s.d0t, s.cv(tk)))(c0, tmkey))
            yield
            s.op("vector", (lambda c0_: lambda e: e.tensor_add(
                c0_, c0_, halo))(c0))
            yield
        s.op("vector", lambda e: e.tensor_copy(halo, xh3[:, :, 511]))
        yield
        for (dst3_, tmkey) in tm_items[1:]:
            s.op("vector", (lambda ds, tk: lambda e: e.tensor_mul(
                ds[:, :, 1:512], first3[:, :, 1:512],
                bci(s.cv(tk), 511)))(dst3_, tmkey))
            yield
        tk0 = tm_items[0][1]
        s.op("vector", (lambda tk: lambda e: e.tensor_mul(
            first3[:, :, 1:512], first3[:, :, 1:512],
            bci(s.cv(tk), 511)))(tk0))
        yield
        for (dst3_, tmkey) in tm_items:
            s.op("vector", (lambda ds: lambda e: e.tensor_add(
                ds[:, :, 1:512], ds[:, :, 1:512], xh3[:, :, 0:511]))(dst3_),
                inc="tv")
            yield

    @staticmethod
    def drip(gen, n):
        for _ in range(n):
            try:
                next(gen)
            except StopIteration:
                return False
        return True

    def ffn_gu(self, f_range, tv_xn, state):
        s = self
        for f in f_range:
            par = f % 2
            if f % 2 == 0:
                if f == 0 and s.pending_gu is not None:
                    state["jg"], state["ju"] = s.pending_gu
                    s.pending_gu = None
                else:
                    state["jg"] = s.slab_dma("wg", f // 2)
                    state["ju"] = s.slab_dma("wu", f // 2)
            jg, ju = state["jg"], state["ju"]
            ringg, ringu = s.ring[jg["slot"]], s.ring[ju["slot"]]
            fd = f % 2
            pairsg = [(ringg[:, kc * 256 + fd * 128: kc * 256 + fd * 128 + 128],
                       s.xn[:, kc * T:(kc + 1) * T]) for kc in range(16)]
            gidg = s.mm_group(("A", par), s.pA[par], pairsg, [jg],
                              extra_waits=[("tv", tv_xn)])
            ta_g = s.evac("ta", ("A", par), gidg,
                          (lambda par_, f_: lambda e: e.activation(
                              s.gatet[par_], s.pA[par_], AF.Silu,
                              scale=s.cv("sgate", f_)))(par, f),
                          extra_waits=([("tv", state["gate_free"][par])]
                                       if par in state["gate_free"] else ()))
            pairsu = [(ringu[:, kc * 256 + fd * 128: kc * 256 + fd * 128 + 128],
                       s.xn[:, kc * T:(kc + 1) * T]) for kc in range(16)]
            gidu = s.mm_group(("B", par), s.pB[par], pairsu, [ju])
            fs = s.fuse[:, (f % 32) * T:((f % 32) + 1) * T]
            tvf = s.evac("tv", ("B", par), gidu,
                         (lambda par_, f_, fs_: lambda e: e.scalar_tensor_tensor(
                             fs_, s.pB[par_], s.cv("sup", f_),
                             s.gatet[par_], ALU.mult, ALU.mult))(par, f, fs),
                         extra_waits=[("ta", ta_g)])
            state["gate_free"][par] = tvf

    def ffn_down(self, half, tv_fuse, first_extra=()):
        s = self
        wname = "wd_lo" if half == 0 else "wd_hi"
        for dout in range(16):
            par = dout % 2
            jl = s.slab_dma(wname, dout)
            rl = s.ring[jl["slot"]]
            pairs = [(rl[:, kc * 128:(kc + 1) * 128],
                      s.fuse[:, kc * T:(kc + 1) * T]) for kc in range(32)]
            gid = s.mm_group(("A", par), s.pA[par], pairs, [jl],
                             extra_waits=[("tv", tv_fuse)])
            ew = list(first_extra) if dout == 0 else []
            if half == 0:
                x_bf = s.xh[:, 0:8192]
                emit = (lambda par_, d_: lambda e: e.scalar_tensor_tensor(
                    s.acc[:, d_ * T:(d_ + 1) * T], s.pA[par_],
                    s.cv("sdown", d_), x_bf[:, d_ * T:(d_ + 1) * T],
                    ALU.mult, ALU.add))(par, dout)
            else:
                emit = (lambda par_, d_: lambda e: e.scalar_tensor_tensor(
                    s.acc[:, d_ * T:(d_ + 1) * T], s.pA[par_],
                    s.cv("sdown", d_), s.acc[:, d_ * T:(d_ + 1) * T],
                    ALU.mult, ALU.add))(par, dout)
            s.evac("tv", ("A", par), gid, emit, extra_waits=ew)
            yield

    def chunk(self, ci):
        s = self
        s.cur_ci = ci
        c, sq = divmod(ci, s.nseq)
        xn3 = s.xn.rearrange("p (a b) -> p a b", b=T)
        acc3 = s.acc.rearrange("p (a b) -> p a b", b=T)
        xh3 = s.xh.rearrange("p (a b) -> p a b", b=T)
        reg = s.region
        last = s.nchunk * s.nseq - 1

        if ci == 0:
            s.emit_preln(0)
        tv_xn = s.tv_xn_pending
        tp_chunk_start = s.c("tp")

        gu_state = {"gate_free": {}, "jg": None, "ju": None}
        s.ffn_gu(range(0, 32), tv_xn, gu_state)
        tv_fuseA = s.c("tv")

        ew0 = [("qo", 16 * s.nout)] if s.nout > 0 else []
        downA = s.ffn_down(0, tv_fuseA, first_extra=ew0)
        for _ in range(16):
            next(downA)
        tp_downA = s.c("tp")
        tv_downA = s.c("tv")

        mixr3 = reg[:, 0:8192].rearrange("p (a b) -> p a b", b=T)
        mixk3 = reg[:, 8192:16384].rearrange("p (a b) -> p a b", b=T)
        mixv3 = reg[:, 16384:24576].rearrange("p (a b) -> p a b", b=T)
        g1 = None
        for i, f in enumerate(range(32, 64)):
            s.ffn_gu([f], tv_xn, gu_state)
            if i == 1:
                g1 = s.dve_ln_mix(
                    [(mixr3, "tmr"), (mixk3, "tmk"), (mixv3, "tmv")],
                    "n1w", "n1b", s.h1halo[sq], None, None, tp_chunk_start,
                    no_mean=True)
            if g1 is not None:
                s.drip(g1, 1)
        while s.drip(g1, 64):
            pass
        tv_mix = s.c("tv")
        tv_fuseB = max(gu_state["gate_free"].values())

        mixdst = {"r": reg[:, 0:8192], "k": reg[:, 8192:16384],
                  "v": reg[:, 16384:24576]}
        ring_rkv = {}
        rsig_free = {}
        kt_free = {}
        for dc in range(NDC):
            par = dc % 2
            if dc % 2 == 0:
                q = dc // 2
                for m in ("r", "k", "v"):
                    tok = s.slab_dma("w" + m, q)
                    ring_rkv[m] = (tok, s.ring[tok["slot"]])
            fd = dc % 2

            def lh(rt, kc):
                return rt[:, kc * 256 + fd * 128: kc * 256 + fd * 128 + 128]

            jr, rr = ring_rkv["r"]
            gidr = s.mm_group(("A", par), s.pA[par],
                              [(lh(rr, kc), mixdst["r"][:, kc * T:(kc + 1) * T])
                               for kc in range(16)], [jr],
                              extra_waits=[("tv", tv_mix)])
            s.evac("ta", ("A", par), gidr,
                   (lambda par_: lambda e: e.activation(
                       s.rsig[par_], s.pA[par_], AF.Sigmoid))(par),
                   extra_waits=([("tv", rsig_free[par])]
                                if par in rsig_free else ()))
            jk, rk = ring_rkv["k"]
            gidk = s.mm_group(("B", par), s.pB[par],
                              [(lh(rk, kc), mixdst["k"][:, kc * T:(kc + 1) * T])
                               for kc in range(16)], [jk])
            s.evac("ta", ("B", par), gidk,
                   (lambda par_: lambda e: e.copy(s.kt[par_], s.pB[par_]))(par),
                   extra_waits=([("tv", kt_free[par])]
                                if par in kt_free else ()))
            jv, rv = ring_rkv["v"]
            gidv = s.mm_group(("D", par), s.pD[par],
                              [(lh(rv, kc), mixdst["v"][:, kc * T:(kc + 1) * T])
                               for kc in range(16)], [jv])
            kt_free[par] = s.evac(
                "tv", ("D", par), gidv,
                (lambda par_: lambda e: e.tensor_mul(
                    s.kvt, s.pD[par_], s.kt[par_]))(par),
                extra_waits=[("ta", s.c("ta"))])
            scan_waits = [("ta", s.ta_setup)] if ci == 0 and dc == 0 else ()
            s.op("vector",
                 (lambda d: lambda e: e.tensor_tensor_scan(
                     s.kvt, s.wdec[:, d:d + 1].broadcast_to((128, T)), s.kvt,
                     s.carry[sq][:, d:d + 1], ALU.mult, ALU.add))(dc),
                 waits=scan_waits)
            s.op("vector",
                 (lambda d: lambda e: e.tensor_copy(
                     s.carry[sq][:, d:d + 1], s.kvt[:, T - 1:T]))(dc))
            wkv_dst = s.wkvmix[:, dc * T:(dc + 1) * T]
            rsig_free[par] = s.op(
                "vector",
                (lambda par_, ds: lambda e: e.tensor_mul(
                    ds, s.rsig[par_], s.kvt))(par, wkv_dst), inc="tv")
            if c == 0:
                s.op("vector",
                     (lambda d: lambda e: e.tensor_scalar_mul(
                         s.ut[:, 0:1], s.kvt[:, 0:1], s.efa[:, d:d + 1]))(dc))
                s.op("vector",
                     (lambda par_: lambda e: e.tensor_mul(
                         s.ut[:, 0:1], s.ut[:, 0:1],
                         s.rsig[par_][:, 0:1]))(par))
                rsig_free[par] = s.op(
                    "vector",
                    (lambda ds: lambda e: e.tensor_add(
                        ds[:, 0:1], ds[:, 0:1], s.ut[:, 0:1]))(wkv_dst),
                    inc="tv")
        tv_wkv = s.op("vector", lambda e: e.engine_nop(), inc="tv")

        jo = None
        ringo = None
        for dc in range(NDC):
            par = dc % 2
            if dc % 2 == 0:
                jo = s.slab_dma("wo", dc // 2)
                ringo = s.ring[jo["slot"]]
            fd = dc % 2
            pairs = [(ringo[:, kc * 256 + fd * 128: kc * 256 + fd * 128 + 128],
                      s.wkvmix[:, kc * T:(kc + 1) * T]) for kc in range(16)]
            gid = s.mm_group(("A", par), s.pA[par], pairs, [jo],
                             extra_waits=[("tv", tv_wkv)])
            s.evac("tv", ("A", par), gid,
                   (lambda par_, d: lambda e: e.tensor_add(
                       s.xn[:, d * T:(d + 1) * T], s.pA[par_],
                       s.xn[:, d * T:(d + 1) * T]))(par, dc))
        tv_y = s.c("tv")
        tp_wo = s.c("tp")

        mixed3 = s.wkvmix.rearrange("p (a b) -> p a b", b=T)
        downB = s.ffn_down(1, tv_fuseB)
        g2 = None
        fin2 = None
        tv2 = ta2 = None
        tv_q1_done = s.tv_q_done
        for i in range(16):
            next(downB)
            if i == 0:
                fin2, _ = s.ln_stats(s.xn, reg[:, 0:8192],
                                     [("tv", tv_y), ("tp", tp_wo)])
            if i == 1:
                tv2, ta2 = fin2(extra_copy=[("tv", tv_q1_done)])
                g2 = s.dve_ln_mix(
                    [(mixed3, "cmm")],
                    "n2w", "n2b", s.h2halo[sq], tv2, ta2, tp_wo)
            if g2 is not None:
                s.drip(g2, 2)
        while s.drip(g2, 64):
            pass
        tv_mixed = s.c("tv")
        tv_h2h = s.tv_h_done
        if ci < last:
            s.x_load(ci + 1, waits=[("tv", tv_mixed)])
        tp_downB = s.c("tp")

        rcm = s.fuse[:, 0:8192]
        jc = None
        ringc = None
        for dc in range(NDC):
            par = dc % 2
            if dc % 2 == 0:
                jc = s.slab_dma("wcr", dc // 2)
                ringc = s.ring[jc["slot"]]
            fd = dc % 2
            pairs = [(ringc[:, kc * 256 + fd * 128: kc * 256 + fd * 128 + 128],
                      s.wkvmix[:, kc * T:(kc + 1) * T]) for kc in range(16)]
            gid = s.mm_group(("A", par), s.pA[par], pairs, [jc],
                             extra_waits=[("tv", tv_mixed)])
            s.evac("ta", ("A", par), gid,
                   (lambda par_, d: lambda e: e.activation(
                       rcm[:, d * T:(d + 1) * T], s.pA[par_],
                       AF.Sigmoid))(par, dc))

        tv_accy = s.op("vector", lambda e: e.tensor_add(acc3, acc3, xn3),
                       inc="tv")

        jk2 = None
        ringk2 = None
        for f in range(48):
            par = f % 2
            if f % 2 == 0:
                jk2 = s.slab_dma("wck", f // 2)
                ringk2 = s.ring[jk2["slot"]]
            fd = f % 2
            pairs = [(ringk2[:, kc * 256 + fd * 128: kc * 256 + fd * 128 + 128],
                      s.wkvmix[:, kc * T:(kc + 1) * T]) for kc in range(16)]
            gid = s.mm_group(("A", par), s.pA[par], pairs, [jk2],
                             extra_waits=[("tv", tv_mixed)])
            s.evac("ta", ("A", par), gid,
                   (lambda par_, f_: lambda e: e.activation(
                       reg[:, f_ * T:(f_ + 1) * T], s.pA[par_],
                       AF.Square))(par, f),
                   extra_waits=[("tv", max(tv_h2h, tv_accy))]
                   if f < 2 else ())
        ta_kcm = s.c("ta")
        tp_wck = s.c("tp")

        fin_next = None
        for dout in range(NDC):
            par = dout % 2
            jvl = s.slab_dma("wcv_lo", dout, used_elems=3072)
            rvl = s.ring[jvl["slot"]]
            pairs1 = [(rvl[:, kc * 128:(kc + 1) * 128],
                       reg[:, kc * T:(kc + 1) * T]) for kc in range(24)]
            s.mm_group(("A", par), s.pA[par], pairs1, [jvl],
                       extra_waits=[("ta", ta_kcm)],
                       close_stop=False, mark_slabs=[])
            jvh = s.slab_dma("wcv_hi", dout, used_elems=3072)
            rvh = s.ring[jvh["slot"]]
            pairs2 = [(rvh[:, kc * 128:(kc + 1) * 128],
                       reg[:, (24 + kc) * T:(25 + kc) * T]) for kc in range(24)]
            gid = s.mm_group(("A", par), s.pA[par], pairs2, [jvh],
                             open_start=False, mark_slabs=[jvl, jvh])
            s.op("vector",
                 (lambda par_, d: lambda e: e.tensor_mul(
                     s.t3, s.pA[par_], rcm[:, d * T:(d + 1) * T]))(par, dout),
                 waits=[("tp", gid)])
            tvv = s.op("vector",
                       (lambda d: lambda e: e.tensor_add(
                           s.acc[:, d * T:(d + 1) * T],
                           s.acc[:, d * T:(d + 1) * T], s.t3))(dout),
                       inc="tv")
            s.bank_free[("A", par)] = ("tv", tvv)
            if ci < last:
                if dout == 0:
                    fin_next, fin_r2n = s.ln_stats(
                        s.xh[:, 0:8192], s.wkvmix[:, 0:8192],
                        [("qx", 16 * (ci + 2)), ("tp", tp_wck)],
                        extra_stat=(), stage2=True)
                elif dout == 4:
                    tv1n, ta1n = fin_next(extra_copy=[("tv", s.tv_q_done)])
                    x3n = s.xh[:, 0:8192].rearrange("p (a b) -> p a b", b=T)

                    def bc3(t):
                        return t.unsqueeze(1).broadcast_to((128, NDC, T))
                    s.op("vector",
                         lambda e: e.tensor_mul(xn3, x3n, bc3(s.rB)),
                         waits=())
                    s.tv_xn_pending = s.op(
                        "vector",
                        lambda e: e.tensor_sub(xn3, xn3, bc3(s.mrB)),
                        waits=[("ta", ta1n)], inc="tv")
                    fin_r2n()
        tv_done = s.c("tv")

        if ci < last:
            s.pending_gu = (s.slab_dma("wg", 0), s.slab_dma("wu", 0))

        s.op("sync",
             (lambda c_, sq_: lambda e: e.dma_start(
                 out=s.out5[sq_, c_], in_=s.acc))(c, sq),
             waits=[("tv", tv_done)], inc16="qo")
        s.nout += 1

    def build(self):
        self.setup()
        for ci in range(self.nchunk * self.nseq):
            self.chunk(ci)
        qo = self.c("qo")
        self.op("sync", lambda e: e.nop(), waits=[("qo", qo)])
        return self.finalize()


def host_inputs(inputs, nchunk=4, nseq=NSEQ, cores=NCORE):
    x = np.asarray(inputs["L0_attn_x"])
    seqlen = nchunk * T

    wcv = np.asarray(inputs["L0_attn_Wcv"])
    wd = np.asarray(inputs["L0_ffn_w_down"])
    wb = {
        "wr": _block_w(np.asarray(inputs["L0_attn_Wr"]), 256),
        "wk": _block_w(np.asarray(inputs["L0_attn_Wk"]), 256),
        "wv": _block_w(np.asarray(inputs["L0_attn_Wv"]), 256),
        "wo": _block_w(np.asarray(inputs["L0_attn_Wo"]), 256),
        "wcr": _block_w(np.asarray(inputs["L0_attn_Wcr"]), 256),
        "wck": _block_w(np.asarray(inputs["L0_attn_Wck"]), 256),
        "wcv_lo": _block_w(wcv[:3072], 128),
        "wcv_hi": _block_w(wcv[3072:], 128),
        "wg": _block_w(np.asarray(inputs["L0_ffn_w_gate"]), 256),
        "wu": _block_w(np.asarray(inputs["L0_ffn_w_up"]), 256),
        "wd_lo": _block_w(wd[:4096], 128),
        "wd_hi": _block_w(wd[4096:], 128),
    }
    consts = np.concatenate([
        _vec128(inputs["L0_attn_norm1_w"]), _vec128(inputs["L0_attn_norm1_b"]),
        _vec128(inputs["L0_attn_norm2_w"]), _vec128(inputs["L0_attn_norm2_b"]),
        _vec128(inputs["L0_attn_time_decay"]),
        _vec128(inputs["L0_attn_time_faaaa"]),
        _vec128(inputs["L0_attn_time_mix_r"]),
        _vec128(inputs["L0_attn_time_mix_k"]),
        _vec128(inputs["L0_attn_time_mix_v"]),
        _vec128(inputs["L0_attn_cm_mix"]),
        _vec128(inputs["L0_ffn_s_down"]),
        _vec128(inputs["L0_ffn_s_gate"]), _vec128(inputs["L0_ffn_s_up"]),
    ], axis=1).astype(np.float32)
    consts = np.ascontiguousarray(consts)
    onesb = np.ones((128, 1), dtype=bfdt)
    onesr = np.ones((1, 128), dtype=np.float32)

    in_maps = []
    for core in range(cores):
        xs = x[core * nseq:(core + 1) * nseq, :seqlen]
        x5 = np.ascontiguousarray(
            xs.reshape(nseq, nchunk, T, NDC, 128).transpose(0, 1, 4, 3, 2)
            .reshape(nseq, nchunk, 128, NDC * T))
        m = {"x5": x5, "consts": consts, "onesb": onesb, "onesr": onesr}
        m.update(wb)
        in_maps.append(m)
    return in_maps


def host_gather(results, nchunk=4, nseq=NSEQ):
    outs = []
    for r in results:
        o = np.asarray(r["out5"])
        o = o.reshape(nseq, nchunk, 128, NDC, T).transpose(0, 1, 4, 3, 2)
        outs.append(o.reshape(nseq, nchunk * T, D))
    return np.concatenate(outs, axis=0)


def kernel(**inputs):
    b = Builder(nchunk=4, nseq=NSEQ)
    nc = b.build()
    in_maps = host_inputs(inputs)
    res = run_bass_kernel_spmd(nc, in_maps, core_ids=list(range(NCORE)))
    return host_gather(res.results).astype(np.float32)
